# version 08
# baseline (speedup 1.0000x reference)
import os
import numpy as np
import ml_dtypes
import concourse.bass as bass
import concourse.mybir as mybir
from concourse.bass_utils import run_bass_kernel_spmd

F32 = mybir.dt.float32
BF16 = mybir.dt.bfloat16
AF = mybir.ActivationFunctionType
ALU = mybir.AluOpType
AX = mybir.AxisListType

MAGIC = 12582912.0
ESHIFT = -8.0
SUMCLAMP = 2.0 ** -40

D = 1024
DC = 8
T = 2048
TH = 2052
QT = 124
NJ = 17
QLAST = T - QT * (NJ - 1)
IFF = 2816
NIC = 22
NCORE = 8
BF16NP = ml_dtypes.bfloat16


def _mask_host(parity):
    m = np.zeros((NJ, 128, QT), np.float32)
    r = np.arange(128)[:, None]
    qc = np.arange(QT)[None, :]
    for j in range(NJ):
        qn = QT if j < NJ - 1 else QLAST
        band = (r - qc >= 0) & (r - qc <= 4) & (qc < qn)
        gk = parity * T - 2 + QT * j + r + 0 * qc
        valid = (gk >= 0) & (gk < 4096)
        m[j] = (band & valid).astype(np.float32)
    return np.ascontiguousarray(m.transpose(1, 0, 2).reshape(128, NJ * QT)).astype(BF16NP)


class Sched:

    ENGS = ("sync", "tensor", "vector", "scalar", "gpsimd")
    SPARSE = ("tM", "vV", "sS")
    SPARSE_ON = True

    def __init__(self):
        self.q = {e: [] for e in self.ENGS}
        self.val = {}
        self.waited = {}

    def op(self, eng, fn, sem=None, n=1):
        if sem is not None:
            self.val[sem] = self.val.get(sem, 0) + n
            self.q[eng].append(("op", fn, sem, n, self.val[sem]))
        else:
            self.q[eng].append(("op", fn, None, 0, 0))

    def wait(self, eng, sem, v=None):
        v = self.val.get(sem, 0) if v is None else v
        if v > 0:
            self.waited.setdefault(sem.name if hasattr(sem, "name") else id(sem), set()).add(v)
            self.q[eng].append(("wait", sem, v))

    def cnt(self, sem):
        return self.val.get(sem, 0)

    def emit(self, eng, engobj):
        trace = os.environ.get("SCHED_TRACE") == eng
        last_cum = {}
        for ent in self.q[eng]:
            if ent[0] == "op" and ent[2] is not None:
                last_cum[id(ent[2])] = ent[4]
        pending = {}
        for i, ent in enumerate(self.q[eng]):
            if ent[0] == "wait":
                _, sem, v = ent
                engobj.wait_ge(sem, v)
                if trace:
                    print(f"[{eng} {i}] WAIT {getattr(sem, 'name', '?')} >= {v}")
                continue
            _, fn, sem, n, cum = ent
            inst = fn(engobj)
            if sem is None:
                if trace:
                    print(f"[{eng} {i}] op (nosem)")
                continue
            key = sem.name if hasattr(sem, "name") else id(sem)
            sparse = self.SPARSE_ON and getattr(sem, "name", "") in self.SPARSE
            if not sparse:
                inst.then_inc(sem, n)
                if trace:
                    print(f"[{eng} {i}] op inc {key} +{n} (dense, cum={cum})")
                continue
            pend = pending.get(id(sem), 0) + n
            wset = self.waited.get(key, set())
            if cum in wset or cum == last_cum[id(sem)]:
                inst.then_inc(sem, pend)
                if trace:
                    print(f"[{eng} {i}] op inc {key} +{pend} (cum={cum})")
                pend = 0
            else:
                if trace:
                    print(f"[{eng} {i}] op ---- {key} (cum={cum})")
            pending[id(sem)] = pend


class _StopBuild(Exception):
    pass


def build_nc(dbg=(), races=False, stop_phase=None, dskip=(), pad_sync=0, pad_kind="dma"):
    nc = bass.Bass("TRN2", target_bir_lowering=False,
                   detect_race_conditions=races)

    xd = nc.dram_tensor("x", [TH, D], F32, kind="ExternalInput").ap()
    wqT = nc.dram_tensor("wqT", [D, D], F32, kind="ExternalInput").ap()
    wkT = nc.dram_tensor("wkT", [D, D], F32, kind="ExternalInput").ap()
    wvT = nc.dram_tensor("wvT", [D, D], F32, kind="ExternalInput").ap()
    woT = nc.dram_tensor("woT", [D, D], F32, kind="ExternalInput").ap()
    bqf = nc.dram_tensor("bqf", [128, 8], F32, kind="ExternalInput").ap()
    bkf = nc.dram_tensor("bkf", [128, 8], F32, kind="ExternalInput").ap()
    bof = nc.dram_tensor("bof", [128, 8], F32, kind="ExternalInput").ap()
    bvr = nc.dram_tensor("bvr", [1, D], F32, kind="ExternalInput").ap()
    wgd = nc.dram_tensor("wg", [IFF, D], F32, kind="ExternalInput").ap()
    wud = nc.dram_tensor("wu", [IFF, D], F32, kind="ExternalInput").ap()
    wdd = nc.dram_tensor("wd", [D, IFF], F32, kind="ExternalInput").ap()
    clips = nc.dram_tensor("clips", [1, 2], F32, kind="ExternalInput").ap()
    identd = nc.dram_tensor("ident", [128, 128], BF16, kind="ExternalInput").ap()
    identfd = nc.dram_tensor("identf", [128, 128], F32, kind="ExternalInput").ap()
    onesd = nc.dram_tensor("onesr", [1, 128], F32, kind="ExternalInput").ap()
    maskd = nc.dram_tensor("mask", [128, NJ * QT], BF16, kind="ExternalInput").ap()
    outd = nc.dram_tensor("out", [T, D], F32, kind="ExternalOutput").ap()

    u_dram = nc.dram_tensor("u_scr", [IFF, T], F32)
    s_dram = nc.dram_tensor("s_scr", [128, DC * T], BF16)
    sg_dram = nc.dram_tensor("sg_scr", [128, DC * T], F32)
    scr_g = nc.dram_tensor("scr_g", [IFF, D], BF16)
    scr_u = nc.dram_tensor("scr_u", [IFF, D], BF16)
    scr_d = nc.dram_tensor("scr_d", [D, IFF], BF16)
    cc1i = nc.dram_tensor("cc1i", [1, 1], F32)
    cc1o = nc.dram_tensor("cc1o", [1, 1], F32)
    cc2i = nc.dram_tensor("cc2i", [1, 1], F32)
    cc2o = nc.dram_tensor("cc2o", [1, 1], F32)

    dbgd = {}
    for k in dbg:
        shape, dt = {
            "xn16": ([128, DC * TH], BF16), "q": ([128, 4 * T], BF16),
            "k": ([128, 4 * TH], BF16), "v": ([128, NJ * 520], BF16),
            "attnfm": ([128, DC * T], BF16), "xq": ([128, DC * T], BF16),
            "h": ([128, NIC * T], BF16), "sg": ([128, DC * T], F32),
            "sm1": ([1, 8], F32), "wmax": ([128, NIC], F32),
            "stats": ([128, 7 * 17], F32), "xtm": ([128, 17 * 1024], F32),
            "s": ([128, DC * T], BF16), "wdt": ([128, NIC * 1024], BF16),
            "scrd": ([D, IFF], BF16),
        }[k]
        dbgd[k] = nc.dram_tensor("dbg_" + k, shape, dt, kind="ExternalOutput").ap()

    S = Sched()
    sb_stack, ps_stack = [], []

    def alloc(name, shape, dtype):
        cm = nc.sbuf_tensor(name, shape, dtype)
        h = cm.__enter__()
        sb_stack.append(cm)
        return h

    def sb_free_to(mark):
        while len(sb_stack) > mark:
            sb_stack.pop().__exit__(None, None, None)

    def palloc(name, shape, dtype):
        cm = nc.psum_tensor(name, shape, dtype)
        h = cm.__enter__()
        ps_stack.append(cm)
        return h

    def ps_free_to(mark):
        while len(ps_stack) > mark:
            ps_stack.pop().__exit__(None, None, None)

    sems_cm = [nc.semaphore(n) for n in ("dS", "dG", "tM", "vV", "sS", "cC")]
    dS, dG, tM, vV, sS, cC = [cm.__enter__() for cm in sems_cm]

    def ckpt(n):
        fire = stop_phase is not None and n >= stop_phase
        if n == 299 and stop_phase is not None and 300 <= stop_phase < 400:
            fire = True
        if fire:
            for eng in ("sync", "gpsimd"):
                S.wait(eng, tM)
                S.wait(eng, vV)
                S.wait(eng, sS)
            S.wait("sync", dG)
            S.wait("sync", dS)
            S.op("sync", lambda e: e.dma_start(out=outd, in_=u_dram.ap()[0:1024, :]), dS, 16)
            S.wait("sync", dS)
            raise _StopBuild()

    try:
        ident = alloc("ident_s", [128, 128], BF16)
        identf = alloc("identf_s", [128, 128], F32)
        onesr = alloc("ones_s", [1, 128], F32)
        mask = alloc("mask_s", [128, NJ * QT], BF16)
        bq_s = alloc("bq_s", [128, 8], F32)
        bk_s = alloc("bk_s", [128, 8], F32)
        bo_s = alloc("bo_s", [128, 8], F32)
        bvb = alloc("bvb_s", [128, D], F32)
        clp = alloc("clp_s", [1, 2], F32)
        mean_b = alloc("mean_s", [128, 17], F32)
        rstd_b = alloc("rstd_s", [128, 17], F32)
        sum_b = alloc("sumb_s", [128, 17], F32)
        msq_b = alloc("msqb_s", [128, 17], F32)
        amax_b = alloc("amax_s", [128, 17], F32)
        c2_b = alloc("c2_s", [128, 17], F32)
        smaxg = alloc("smaxg_s", [128, NIC], F32)
        smaxu = alloc("smaxu_s", [128, NIC], F32)
        smaxd = alloc("smaxd_s", [128, DC], F32)
        ag_b = alloc("ag_s", [128, NIC], F32)
        au_b = alloc("au_s", [128, NIC], F32)
        agau = alloc("agau_s", [128, NIC], F32)
        ccb = alloc("ccb_s", [128, NIC], F32)
        sdsu = alloc("sdsu_s", [128, DC], F32)
        wmax = alloc("wmax_s", [128, NIC], F32)
        sm1 = alloc("sm1_s", [1, 8], F32)
        smb = alloc("smb_s", [128, 4], F32)
        red1 = alloc("red1_s", [128, 1], F32)
        redr = alloc("redr_s", [1, 128], F32)
        rcp1 = alloc("rcp1_s", [128, 4], F32)
        ceps = alloc("ceps_s", [128, 1], F32)
        cesh = alloc("cesh_s", [128, 1], F32)
        def vfill():
            S.wait("vector", vV, S.cnt(vV))

        psA = len(ps_stack)
        ps = palloc("ps_all", [128, 4096], F32)

        def B(b, c0, n):
            return ps[:, 512 * b + c0:512 * b + c0 + n]

        for dst, src in ((ident, identd), (identf, identfd), (onesr, onesd),
                         (mask, maskd), (bq_s, bqf), (bk_s, bkf), (bo_s, bof),
                         (clp, clips)):
            S.op("sync", lambda e, d=dst, s=src: e.dma_start(out=d[:, :], in_=s), dS, 16)

        xn16 = alloc("xn16", [128, DC * TH], BF16)

        sbB1 = len(sb_stack)
        x_tm = alloc("x_tm", [128, 17 * 1024], F32)
        sq_t = alloc("sq_t", [128, 4 * 1024], F32)
        bvrow = alloc("bvrow_s", [1, D], F32)

        S.op("sync", lambda e: e.dma_start(out=bvrow[:, :], in_=bvr), dS, 16)
        S.op("sync", lambda e: e.dma_start(
            out=bass.AP(x_tm[:, :].tensor, 0, [[17 * 1024, 128], [1024, 16], [1, 1024]]),
            in_=bass.AP(xd.tensor, 2 * D, [[D, 128], [128 * D, 16], [1, D]])), dS, 16)
        S.op("sync", lambda e: e.dma_start(out=x_tm[0:2, 16 * 1024:17 * 1024], in_=xd[0:2, :]), dS, 16)
        S.op("sync", lambda e: e.dma_start(out=x_tm[2:4, 16 * 1024:17 * 1024], in_=xd[2050:2052, :]), dS, 16)
        dS_init = S.cnt(dS)

        S.wait("tensor", dS, dS_init)
        for c in range(2):
            S.op("tensor", lambda e, c=c: e.matmul(
                B(c, 0, 512), onesr[:, :], bvrow[:, 512 * c:512 * c + 512],
                start=True, stop=True), tM, 1)
        S.wait("vector", tM)
        S.op("vector", lambda e: e.tensor_copy(bvb[:, :], ps[:, 0:1024]), vV, 1)
        v_bvb = S.cnt(vV)

        S.op("vector", lambda e: e.memset(ceps[:, :], 1e-5), vV, 1)
        S.op("vector", lambda e: e.memset(cesh[:, :], ESHIFT), vV, 1)
        S.op("vector", lambda e: e.memset(sum_b[:, :], 0.0), vV, 1)
        S.op("vector", lambda e: e.memset(msq_b[:, :], 0.0), vV, 1)
        S.op("vector", lambda e: e.memset(amax_b[:, :], 0.0), vV, 1)
        v_msets = S.cnt(vV)
        S.wait("vector", dS, dS_init)
        S.wait("scalar", dS, dS_init)
        S.wait("scalar", vV, v_msets)

        sq_vq = {}
        for t in range(17):
            pn = 128 if t < 16 else 4
            if t >= 4:
                S.wait("scalar", vV, sq_vq[t - 4])
            S.op("scalar", lambda e, t=t, pn=pn: e.activation(
                sq_t[0:pn, 1024 * (t % 4):1024 * (t % 4) + 1024],
                x_tm[0:pn, 1024 * t:1024 * t + 1024], AF.Square), sS, 1)
            s_here = S.cnt(sS)
            S.op("vector", lambda e, t=t, pn=pn: e.reduce_sum(
                sum_b[0:pn, t:t + 1], x_tm[0:pn, 1024 * t:1024 * t + 1024], AX.X), vV, 1)
            S.wait("vector", sS, s_here)
            S.op("vector", lambda e, t=t, pn=pn: e.reduce_sum(
                msq_b[0:pn, t:t + 1], sq_t[0:pn, 1024 * (t % 4):1024 * (t % 4) + 1024],
                AX.X), vV, 1)
            sq_vq[t] = S.cnt(vV)
        S.op("vector", lambda e: e.tensor_scalar(
            mean_b[:, :], sum_b[:, :], 1.0 / D, None, ALU.mult), vV, 1)
        vfill()
        S.op("vector", lambda e: e.tensor_scalar(
            msq_b[:, :], msq_b[:, :], 1.0 / D, None, ALU.mult), vV, 1)
        vfill()
        S.op("vector", lambda e: e.tensor_mul(rstd_b[:, :], mean_b[:, :], mean_b[:, :]), vV, 1)
        vfill()
        S.op("vector", lambda e: e.tensor_sub(rstd_b[:, :], msq_b[:, :], rstd_b[:, :]), vV, 1)
        if "stats" in dbgd:
            var_snap = alloc("var_snap", [128, 2 * 17], F32)
            S.op("vector", lambda e: e.tensor_copy(var_snap[:, 0:17], rstd_b[:, :]), vV, 1)
        v_var = S.cnt(vV)
        S.wait("scalar", vV, v_var)
        S.op("scalar", lambda e: e.activation(rstd_b[:, :], rstd_b[:, :], AF.Sqrt,
                                              bias=ceps[:, :]), sS, 1)
        s_sqrt = S.cnt(sS)
        if "stats" in dbgd:
            S.wait("vector", sS, s_sqrt)
            S.op("vector", lambda e: e.tensor_copy(var_snap[:, 17:34], rstd_b[:, :]), vV, 1)
        S.wait("vector", sS, s_sqrt)
        S.op("vector", lambda e: e.reciprocal(rstd_b[:, :], rstd_b[:, :]), vV, 1)
        vfill()

        for t in range(17):
            pn = 128 if t < 16 else 4
            S.op("vector", lambda e, t=t, pn=pn: e.tensor_scalar(
                x_tm[0:pn, 1024 * t:1024 * t + 1024], x_tm[0:pn, 1024 * t:1024 * t + 1024],
                mean_b[0:pn, t:t + 1], rstd_b[0:pn, t:t + 1], ALU.subtract, ALU.mult), vV, 1)
            S.op("vector", lambda e, t=t, pn=pn: e.tensor_reduce(
                amax_b[0:pn, t:t + 1], x_tm[0:pn, 1024 * t:1024 * t + 1024], AX.X,
                ALU.max, apply_absolute_value=True), vV, 1)
        v_ln = S.cnt(vV)

        vfill()
        S.op("vector", lambda e: e.tensor_reduce(red1[:, :], amax_b[:, :], AX.X, ALU.max), vV, 1)
        v_red1 = S.cnt(vV)
        S.wait("tensor", vV, v_red1)
        S.op("tensor", lambda e: e.transpose(B(2, 0, 128)[0:1, :], red1[:, :],
                                             identf[:, :]), tM, 1)
        t_red1 = S.cnt(tM)
        S.wait("vector", tM, t_red1)
        S.op("vector", lambda e: e.tensor_reduce(redr[0:1, 0:1], B(2, 0, 128)[0:1, :],
                                                 AX.X, ALU.max), vV, 1)
        S.wait("gpsimd", vV)
        S.op("gpsimd", lambda e: e.dma_start(out=cc1i.ap(), in_=redr[0:1, 0:1]), dG, 16)
        S.wait("gpsimd", dG)
        S.op("gpsimd", lambda e: e.collective_compute(
            "AllReduce", ALU.max, replica_groups=[list(range(NCORE))],
            ins=[cc1i.ap()], outs=[cc1o.ap()]), cC, 1)
        S.wait("gpsimd", cC)
        S.op("gpsimd", lambda e: e.dma_start(out=sm1[0:1, 0:1], in_=cc1o.ap()), dG, 16)
        dG_g1 = S.cnt(dG)

        if "stats" in dbgd or "xtm" in dbgd:
            S.wait("sync", vV, v_ln)
            S.wait("sync", sS, S.cnt(sS))
            if "stats" in dbgd:
                for i, src in enumerate((sum_b, msq_b, mean_b, rstd_b, amax_b)):
                    S.op("sync", lambda e, i=i, src=src: e.dma_start(
                        out=dbgd["stats"][:, 17 * i:17 * i + 17], in_=src[:, :]), dS, 16)
                S.op("sync", lambda e: e.dma_start(
                    out=dbgd["stats"][:, 85:119], in_=var_snap[:, :]), dS, 16)
            if "xtm" in dbgd:
                S.op("sync", lambda e: e.dma_start(out=dbgd["xtm"], in_=x_tm[:, :]), dS, 16)
            for eng in ("vector", "scalar", "tensor", "gpsimd"):
                S.wait(eng, dS)

        ckpt(11)
        S.wait("tensor", vV)
        v_evA = 0
        s_evB = 0
        for t in range(17):
            pn = 128 if t < 16 else 4
            if t >= 1:
                S.wait("tensor", vV, v_evA)
                S.wait("tensor", sS, s_evB)
            for dc in range(DC):
                S.op("tensor", lambda e, t=t, dc=dc, pn=pn: e.transpose(
                    B(dc, 384, pn),
                    x_tm[0:pn, 1024 * t + 128 * dc:1024 * t + 128 * dc + 128],
                    identf[0:pn, 0:pn]), tM, 1)
            t_tr = S.cnt(tM)
            S.wait("vector", tM, t_tr)
            S.wait("scalar", tM, t_tr)
            if t < 16:
                S.op("vector", lambda e, t=t: e.tensor_copy(
                    bass.AP(xn16[:, :].tensor, 2 + 128 * t, [[DC * TH, 128], [TH, 4], [1, 128]]),
                    bass.AP(ps[:, :].tensor, 384, [[4096, 128], [512, 4], [1, 128]])), vV, 1)
                S.op("scalar", lambda e, t=t: e.activation(
                    bass.AP(xn16[:, :].tensor, 4 * TH + 2 + 128 * t, [[DC * TH, 128], [TH, 4], [1, 128]]),
                    bass.AP(ps[:, :].tensor, 2048 + 384, [[4096, 128], [512, 4], [1, 128]]),
                    AF.Identity), sS, 1)
            else:
                S.op("vector", lambda e: e.tensor_copy(
                    bass.AP(xn16[:, :].tensor, 0, [[DC * TH, 128], [TH, 4], [1, 2]]),
                    bass.AP(ps[:, :].tensor, 384, [[4096, 128], [512, 4], [1, 2]])), vV, 1)
                S.op("vector", lambda e: e.tensor_copy(
                    bass.AP(xn16[:, :].tensor, 2050, [[DC * TH, 128], [TH, 4], [1, 2]]),
                    bass.AP(ps[:, :].tensor, 386, [[4096, 128], [512, 4], [1, 2]])), vV, 1)
                S.op("scalar", lambda e: e.activation(
                    bass.AP(xn16[:, :].tensor, 4 * TH + 0, [[DC * TH, 128], [TH, 4], [1, 2]]),
                    bass.AP(ps[:, :].tensor, 2048 + 384, [[4096, 128], [512, 4], [1, 2]]),
                    AF.Identity), sS, 1)
                S.op("scalar", lambda e: e.activation(
                    bass.AP(xn16[:, :].tensor, 4 * TH + 2050, [[DC * TH, 128], [TH, 4], [1, 2]]),
                    bass.AP(ps[:, :].tensor, 2048 + 386, [[4096, 128], [512, 4], [1, 2]]),
                    AF.Identity), sS, 1)
            v_evA = S.cnt(vV)
            s_evB = S.cnt(sS)
        v_xn16 = S.cnt(vV)
        s_xn16 = S.cnt(sS)

        S.op("scalar", lambda e: e.activation(sm1[0:1, 1:2], clp[0:1, 0:1], AF.Sigmoid), sS, 1)
        s_sig = S.cnt(sS)
        S.wait("vector", dG, dG_g1)
        S.wait("vector", sS, s_sig)
        S.op("vector", lambda e: e.scalar_tensor_tensor(
            sm1[0:1, 2:3], sm1[0:1, 0:1], 1.0 / 127.0, sm1[0:1, 1:2],
            ALU.mult, ALU.mult), vV, 1)
        vfill()
        S.op("vector", lambda e: e.reciprocal(sm1[0:1, 3:4], sm1[0:1, 2:3]), vV, 1)
        v_sx = S.cnt(vV)
        S.wait("tensor", vV, v_sx)
        S.wait("tensor", sS, s_xn16)
        S.op("tensor", lambda e: e.matmul(B(7, 0, 2), onesr[:, :], sm1[0:1, 2:4],
                                          start=True, stop=True), tM, 1)
        S.wait("vector", tM)
        S.op("vector", lambda e: e.tensor_copy(smb[:, 0:2], B(7, 0, 2)), vV, 1)
        vfill()
        S.op("vector", lambda e: e.tensor_scalar(
            c2_b[:, :], rstd_b[:, :], smb[:, 1:2], None, ALU.mult), vV, 1)
        v_c2 = S.cnt(vV)
        if "xn16" in dbgd:
            S.wait("sync", vV, v_xn16)
            S.wait("sync", sS, s_xn16)
            S.op("sync", lambda e: e.dma_start(out=dbgd["xn16"], in_=xn16[:, :]), dS, 16)
            for eng in ("vector", "scalar", "tensor", "gpsimd"):
                S.wait(eng, dS)

        ckpt(21)
        sb_free_to(sbB1)

        sbB2 = len(sb_stack)
        wstg = alloc("wstg", [128, 2 * 4096], BF16)
        q_sb = alloc("q_sb", [128, 4 * T], BF16)
        k_sb = alloc("k_sb", [128, 4 * TH], BF16)
        v_sb = alloc("v_sb", [128, NJ * 520], BF16)
        expm = alloc("expm", [128, 2 * 496], BF16)
        att_j = alloc("att_j", [128, 512], F32)
        rec4 = alloc("rec4", [128, 8], F32)
        attn_fm = alloc("attn_fm", [128, DC * T], BF16)
        s_rot = alloc("s_rot", [128, 2 * T], BF16)
        wnat = alloc("wnat", [128, 4 * 1024], F32)
        wqi = alloc("wqi", [128, 4 * 1024], BF16)

        t_xn_done = S.cnt(tM)
        S.wait("gpsimd", tM, t_xn_done)
        S.wait("sync", tM, t_xn_done)
        S.wait("gpsimd", vV, v_xn16)
        S.wait("gpsimd", sS, s_xn16)
        S.wait("sync", vV, v_xn16)
        S.wait("sync", sS, s_xn16)

        def stage_w(src_ap, half, slot):
            S.op("gpsimd", lambda e, s=src_ap, half=half, slot=slot: e.dma_start(
                out=bass.AP(wstg[:, :].tensor, 4096 * slot, [[2 * 4096, 128], [512, 8], [1, 512]]),
                in_=bass.AP(s.tensor, 512 * half, [[D, 128], [128 * D, 8], [1, 512]])), dG, 16)

        nat_jobs = [("g", ic) for ic in range(NIC)] + [("u", ic) for ic in range(NIC)]
        nat_state = {"i": 0, "vq": {}, "dg": {}, "pend": []}

        def nat_slot():
            i = nat_state["i"]
            if i < len(nat_jobs):
                njobs = min(2, len(nat_jobs) - i)
                nat_state["i"] += njobs
                jobs = []
                for k in range(njobs):
                    kind, ic = nat_jobs[i + k]
                    jobs.append((i + k, kind, ic, (i + k) % 4))
                for idx, kind, ic, sl in jobs:
                    src = wgd if kind == "g" else wud
                    if idx >= 4:
                        S.wait("sync", vV, nat_state["vq"][idx - 4])
                    S.op("sync", lambda e, src=src, ic=ic, sl=sl: e.dma_start(
                        out=wnat[:, 1024 * sl:1024 * sl + 1024],
                        in_=src[128 * ic:128 * ic + 128, :]), dS, 16)
                nat_state["pend"].append((jobs, S.cnt(dS)))
            if not nat_state["pend"]:
                return
            if len(nat_state["pend"]) < 2 and nat_state["i"] < len(nat_jobs):
                return
            jobs, d_here = nat_state["pend"].pop(0)
            njobs = len(jobs)
            S.wait("vector", dS, d_here)
            for idx, kind, ic, sl in jobs:
                if idx >= 4:
                    S.wait("vector", dG, nat_state["dg"][idx - 4])
            for idx, kind, ic, sl in jobs:
                smax = smaxg if kind == "g" else smaxu
                S.op("vector", lambda e, smax=smax, ic=ic, sl=sl: e.tensor_reduce(
                    smax[:, ic:ic + 1], wnat[:, 1024 * sl:1024 * sl + 1024], AX.X,
                    ALU.max, apply_absolute_value=True), vV, 1)
            if njobs == 1:
                vfill()
            for idx, kind, ic, sl in jobs:
                smax = smaxg if kind == "g" else smaxu
                S.op("vector", lambda e, smax=smax, ic=ic, sl=sl: e.reciprocal(
                    rcp1[:, sl:sl + 1], smax[:, ic:ic + 1]), vV, 1)
            vfill()
            for idx, kind, ic, sl in jobs:
                S.op("vector", lambda e, sl=sl: e.tensor_scalar(
                    rcp1[:, sl:sl + 1], rcp1[:, sl:sl + 1], 127.0, None, ALU.mult), vV, 1)
            vfill()
            for idx, kind, ic, sl in jobs:
                S.op("vector", lambda e, sl=sl: e.tensor_scalar(
                    wnat[:, 1024 * sl:1024 * sl + 1024], wnat[:, 1024 * sl:1024 * sl + 1024],
                    rcp1[:, sl:sl + 1], MAGIC, ALU.mult, ALU.add), vV, 1)
            for idx, kind, ic, sl in jobs:
                S.op("vector", lambda e, sl=sl: e.tensor_scalar(
                    wqi[:, 1024 * sl:1024 * sl + 1024], wnat[:, 1024 * sl:1024 * sl + 1024],
                    -MAGIC, None, ALU.add), vV, 1)
            vq_here = S.cnt(vV)
            S.wait("gpsimd", vV, vq_here)
            for idx, kind, ic, sl in jobs:
                scr = scr_g if kind == "g" else scr_u
                nat_state["vq"][idx] = vq_here
                S.op("gpsimd", lambda e, scr=scr, ic=ic, sl=sl: e.dma_start(
                    out=scr.ap()[128 * ic:128 * ic + 128, :],
                    in_=wqi[:, 1024 * sl:1024 * sl + 1024]), dG, 16)
                nat_state["dg"][idx] = S.cnt(dG)

        v_attn_done = 0
        attn_evac = 0
        for half in range(2):
            S.wait("gpsimd", tM)
            stage_w(wqT, half, 0)
            S.wait("tensor", dG)
            S.wait("tensor", vV, v_xn16)
            S.wait("tensor", sS, s_xn16)
            for ec in range(4):
                for dc in range(DC):
                    for t4 in range(4):
                        S.op("tensor", lambda e, ec=ec, dc=dc, t4=t4: e.matmul(
                            B(t4, 0, 512),
                            wstg[:, 512 * dc + 128 * ec:512 * dc + 128 * ec + 128],
                            xn16[:, TH * dc + 2 + 512 * t4:TH * dc + 2 + 512 * t4 + 512],
                            start=(dc == 0), stop=(dc == DC - 1)), tM, 1)
                t_q = S.cnt(tM)
                S.wait("scalar", tM, t_q)
                S.op("scalar", lambda e, ec=ec, half=half: e.activation(
                    q_sb[:, T * ec:T * ec + T], ps[:, 0:2048], AF.Identity,
                    bias=bq_s[:, 4 * half + ec:4 * half + ec + 1]), sS, 1)
                S.wait("tensor", sS)
            t_qdone = S.cnt(tM)
            stage_w(wkT, half, 1)
            S.wait("tensor", dG)
            for ec in range(4):
                for dc in range(DC):
                    for t4 in range(4):
                        S.op("tensor", lambda e, ec=ec, dc=dc, t4=t4: e.matmul(
                            B(t4, 0, 512),
                            wstg[:, 4096 + 512 * dc + 128 * ec:4096 + 512 * dc + 128 * ec + 128],
                            xn16[:, TH * dc + 512 * t4:TH * dc + 512 * t4 + 512],
                            start=(dc == 0), stop=(dc == DC - 1)), tM, 1)
                    S.op("tensor", lambda e, ec=ec, dc=dc: e.matmul(
                        ps[:, 2048:2052],
                        wstg[:, 4096 + 512 * dc + 128 * ec:4096 + 512 * dc + 128 * ec + 128],
                        xn16[:, TH * dc + 2048:TH * dc + 2052],
                        start=(dc == 0), stop=(dc == DC - 1)), tM, 1)
                t_k = S.cnt(tM)
                S.wait("scalar", tM, t_k)
                S.op("scalar", lambda e, ec=ec, half=half: e.activation(
                    k_sb[:, TH * ec:TH * ec + 2048], ps[:, 0:2048], AF.Identity,
                    bias=bk_s[:, 4 * half + ec:4 * half + ec + 1]), sS, 1)
                S.op("scalar", lambda e, ec=ec, half=half: e.activation(
                    k_sb[:, TH * ec + 2048:TH * ec + 2052], ps[:, 2048:2052], AF.Identity,
                    bias=bk_s[:, 4 * half + ec:4 * half + ec + 1]), sS, 1)
                S.wait("tensor", sS)
            S.wait("gpsimd", tM, t_qdone)
            stage_w(wvT, half, 0)
            S.op("vector", lambda e: e.memset(v_sb[:, :], 1.0), vV, 1)
            v_vset = S.cnt(vV)
            S.wait("vector", vV, v_bvb)
            S.wait("tensor", vV, v_vset)
            S.wait("tensor", dG)
            for j in range(NJ):
                zone = j % 4
                kn = min(128, TH - QT * j)
                if j >= 4:
                    S.wait("tensor", vV)
                for dc in range(DC):
                    S.op("tensor", lambda e, j=j, dc=dc, zone=zone, kn=kn: e.matmul(
                        B(zone, 0, 512)[0:kn, :],
                        xn16[:, TH * dc + QT * j:TH * dc + QT * j + kn],
                        wstg[:, 512 * dc:512 * dc + 512],
                        start=(dc == 0), stop=(dc == DC - 1)), tM, 1)
                t_v = S.cnt(tM)
                S.wait("vector", tM, t_v)
                S.op("vector", lambda e, j=j, zone=zone, half=half, kn=kn: e.tensor_tensor(
                    bass.AP(v_sb[:, :].tensor, 520 * j, [[NJ * 520, kn], [65, 8], [1, 64]]),
                    bass.AP(ps[:, :].tensor, 512 * zone, [[4096, kn], [64, 8], [1, 64]]),
                    bass.AP(bvb[:, :].tensor, 512 * half, [[D, kn], [64, 8], [1, 64]]),
                    ALU.add), vV, 1)
            v_vdone = S.cnt(vV)

            if half == 0:
                ckpt(25)
            S.wait("tensor", vV, v_vdone)
            for j in range(NJ):
                qn = QT if j < NJ - 1 else QLAST
                kn = min(128, TH - QT * j)
                p = j % 2
                if attn_evac:
                    S.wait("tensor", vV, attn_evac)
                t_sc0 = 0
                for g in range(2):
                    for hg in range(4):
                        hh = 4 * g + hg
                        ec, pb = hh // 2, 64 * (hh % 2)
                        bank = 4 * g + hg
                        S.op("tensor", lambda e, j=j, ec=ec, pb=pb, bank=bank, p=p, qn=qn, kn=kn: e.matmul(
                            B(bank, 124 * p, qn)[0:kn, :],
                            k_sb[pb:pb + 64, TH * ec + QT * j:TH * ec + QT * j + kn],
                            q_sb[pb:pb + 64, T * ec + QT * j:T * ec + QT * j + qn],
                            start=True, stop=True), tM, 1)
                    if g == 0:
                        t_sc0 = S.cnt(tM)
                t_sc1 = S.cnt(tM)
                S.wait("scalar", tM, t_sc0)
                for hg in range(4):
                    S.op("scalar", lambda e, hg=hg, p=p, qn=qn: e.activation(
                        expm[:, QT * hg:QT * hg + qn], B(hg, 124 * p, qn),
                        AF.Exp, bias=cesh[:, :], scale=0.125), sS, 1)
                s_exp0 = S.cnt(sS)
                S.wait("scalar", tM, t_sc1)
                for hg in range(4):
                    S.op("scalar", lambda e, hg=hg, p=p, qn=qn: e.activation(
                        expm[:, 496 + QT * hg:496 + QT * hg + qn], B(4 + hg, 124 * p, qn),
                        AF.Exp, bias=cesh[:, :], scale=0.125), sS, 1)
                s_exp1 = S.cnt(sS)
                S.wait("vector", sS, s_exp0)
                S.op("vector", lambda e, j=j, qn=qn: e.tensor_tensor(
                    bass.AP(expm[:, :].tensor, 0, [[2 * 496, 128], [QT, 4], [1, qn]]),
                    bass.AP(expm[:, :].tensor, 0, [[2 * 496, 128], [QT, 4], [1, qn]]),
                    bass.AP(mask[:, :].tensor, QT * j, [[NJ * QT, 128], [0, 4], [1, qn]]),
                    ALU.mult), vV, 1)
                v_m0 = S.cnt(vV)
                S.wait("vector", sS, s_exp1)
                S.op("vector", lambda e, j=j, qn=qn: e.tensor_tensor(
                    bass.AP(expm[:, :].tensor, 496, [[2 * 496, 128], [QT, 4], [1, qn]]),
                    bass.AP(expm[:, :].tensor, 496, [[2 * 496, 128], [QT, 4], [1, qn]]),
                    bass.AP(mask[:, :].tensor, QT * j, [[NJ * QT, 128], [0, 4], [1, qn]]),
                    ALU.mult), vV, 1)
                v_m1 = S.cnt(vV)
                S.wait("tensor", vV, v_m0)
                for hg in range(4):
                    S.op("tensor", lambda e, j=j, hg=hg, p=p, qn=qn: e.matmul(
                        B(hg, 248 + 68 * p, 65)[0:qn, :],
                        expm[:, QT * hg:QT * hg + qn],
                        v_sb[:, 520 * j + 65 * hg:520 * j + 65 * hg + 65],
                        start=True, stop=True), tM, 1)
                t_av0 = S.cnt(tM)
                S.wait("tensor", vV, v_m1)
                for hg in range(4):
                    S.op("tensor", lambda e, j=j, hg=hg, p=p, qn=qn: e.matmul(
                        B(4 + hg, 248 + 68 * p, 65)[0:qn, :],
                        expm[:, 496 + QT * hg:496 + QT * hg + qn],
                        v_sb[:, 520 * j + 65 * (4 + hg):520 * j + 65 * (4 + hg) + 65],
                        start=True, stop=True), tM, 1)
                t_av1 = S.cnt(tM)
                for g, t_av in ((0, t_av0), (1, t_av1)):
                    S.wait("vector", tM, t_av)
                    off = 2048 * g + 248 + 68 * p + 64
                    S.op("vector", lambda e, g=g, off=off, qn=qn: e.tensor_scalar(
                        rec4[0:qn, 4 * g:4 * g + 4],
                        bass.AP(ps[:, :].tensor, off, [[4096, qn], [512, 4]]),
                        SUMCLAMP, None, ALU.max), vV, 1)
                vfill()
                for g in range(2):
                    S.op("vector", lambda e, g=g, qn=qn: e.reciprocal(
                        rec4[0:qn, 4 * g:4 * g + 4], rec4[0:qn, 4 * g:4 * g + 4]), vV, 1)
                vfill()
                for g in range(2):
                    for hg in range(4):
                        hh = 4 * g + hg
                        S.op("vector", lambda e, g=g, hg=hg, hh=hh, p=p, qn=qn: e.tensor_scalar(
                            att_j[0:qn, 64 * hh:64 * hh + 64],
                            B(4 * g + hg, 248 + 68 * p, 64)[0:qn, :],
                            rec4[0:qn, 4 * g + hg:4 * g + hg + 1], None, ALU.mult), vV, 1)
                v_attj = S.cnt(vV)
                S.wait("tensor", vV, v_attj)
                for cg in range(4):
                    S.op("tensor", lambda e, cg=cg, qn=qn: e.transpose(
                        B(cg, 384, qn),
                        att_j[0:qn, 128 * cg:128 * cg + 128], identf[0:qn, 0:qn]), tM, 1)
                t_tp = S.cnt(tM)
                S.wait("vector", tM, t_tp)
                S.op("vector", lambda e, j=j, qn=qn, half=half: e.tensor_copy(
                    bass.AP(attn_fm[:, :].tensor, T * 4 * half + QT * j,
                            [[DC * T, 128], [T, 4], [1, qn]]),
                    bass.AP(ps[:, :].tensor, 384, [[4096, 128], [512, 4], [1, qn]])), vV, 1)
                attn_evac = S.cnt(vV)
                nat_slot()
                nat_slot()
            v_attn_done = S.cnt(vV)
            if half == 0:
                ckpt(26)

            dumped = False
            for k_, buf in (("q", q_sb), ("k", k_sb), ("v", v_sb)):
                if k_ in dbgd and half == 0:
                    S.wait("sync", vV, v_attn_done)
                    S.op("sync", lambda e, b=buf, k_=k_: e.dma_start(out=dbgd[k_], in_=b[:, :]), dS, 16)
                    dumped = True
            if dumped:
                for eng in ("vector", "scalar", "tensor", "gpsimd"):
                    S.wait(eng, dS)

        if "attnfm" in dbgd:
            S.wait("sync", vV, v_attn_done)
            S.op("sync", lambda e: e.dma_start(out=dbgd["attnfm"], in_=attn_fm[:, :]), dS, 16)
            for eng in ("vector", "scalar", "tensor", "gpsimd"):
                S.wait(eng, dS)

        ckpt(31)
        dS_spill = {}
        for half in range(2):
            S.wait("gpsimd", tM)
            stage_w(woT, half, 1)
            S.wait("tensor", dG)
            S.wait("tensor", vV, v_attn_done)
            for ec in range(4):
                ecg = 4 * half + ec
                for dc in range(DC):
                    for t4 in range(4):
                        S.op("tensor", lambda e, ec=ec, dc=dc, t4=t4: e.matmul(
                            B(t4, 0, 512),
                            wstg[:, 4096 + 512 * dc + 128 * ec:4096 + 512 * dc + 128 * ec + 128],
                            attn_fm[:, T * dc + 512 * t4:T * dc + 512 * t4 + 512],
                            start=(dc == 0), stop=(dc == DC - 1)), tM, 1)
                t_wo = S.cnt(tM)
                S.wait("scalar", tM, t_wo)
                if ecg >= 2:
                    S.wait("scalar", dS, dS_spill.get(ecg - 2, 0))
                S.op("scalar", lambda e, ecg=ecg: e.activation(
                    s_rot[:, T * (ecg % 2):T * (ecg % 2) + T], ps[:, 0:2048],
                    AF.Identity, bias=bo_s[:, ecg:ecg + 1]), sS, 1)
                s_here = S.cnt(sS)
                S.wait("sync", sS, s_here)
                S.op("sync", lambda e, ecg=ecg: e.dma_start(
                    out=s_dram.ap()[:, T * ecg:T * ecg + T],
                    in_=s_rot[:, T * (ecg % 2):T * (ecg % 2) + T]), dS, 16)
                dS_spill[ecg] = S.cnt(dS)
                S.wait("tensor", sS)
        t_wo_all = S.cnt(tM)
        s_wo_all = S.cnt(sS)
        dS_s_spilled = S.cnt(dS)

        ckpt(41)
        sb_free_to(sbB2 - 1)

        sbB3 = len(sb_stack)
        xq_fm = alloc("xq_fm", [128, DC * T], BF16)
        xrl = alloc("xrl", [128, 2 * 1024], F32)
        xq_t = alloc("xq_t", [128, 1024], F32)
        wgT_s = alloc("wgT_s", [128, DC * IFF], BF16)
        wuT_s = alloc("wuT_s", [128, DC * IFF], BF16)
        sig_b = alloc("sig_b", [128, 2048], F32)
        g_sb = alloc("g_sb", [128, 2048], F32)
        uf = alloc("uf", [128, 4 * 1024], F32)
        wdnat = alloc("wdnat", [128, IFF], F32)
        wdi = alloc("wdi", [128, IFF], BF16)

        S.wait("sync", tM, t_wo_all)
        S.wait("sync", dS, dS_s_spilled)
        S.wait("sync", sS, s_wo_all)
        S.wait("gpsimd", tM, t_wo_all)
        S.wait("gpsimd", sS, s_wo_all)

        while nat_state["pend"]:
            nat_slot()
        dG_nat_all = S.cnt(dG)
        S.wait("sync", dG, dG_nat_all)
        wT_hc = {}
        for hc in range(DC):
            S.op("sync", lambda e, hc=hc: e.dma_start(
                out=wgT_s[:, IFF * hc:IFF * hc + IFF],
                in_=scr_g.ap()[:, 128 * hc:128 * hc + 128], transpose=True), dS, 16)
            S.op("sync", lambda e, hc=hc: e.dma_start(
                out=wuT_s[:, IFF * hc:IFF * hc + IFF],
                in_=scr_u.ap()[:, 128 * hc:128 * hc + 128], transpose=True), dS, 16)
            wT_hc[hc] = S.cnt(dS)
        dS_wT = S.cnt(dS)

        S.wait("vector", vV, v_c2)
        S.wait("tensor", sS, s_wo_all)
        t_xq_tp = S.cnt(tM)
        v_xq_ev = 0
        for t in range(16):
            if t >= 2:
                S.wait("gpsimd", vV)
            S.op("gpsimd", lambda e, t=t: e.dma_start(
                out=xrl[:, 1024 * (t % 2):1024 * (t % 2) + 1024],
                in_=xd[2 + 128 * t:2 + 128 * t + 128, :]), dG, 16)
            d_x = S.cnt(dG)
            S.wait("vector", dG, d_x)
            if t >= 1:
                S.wait("vector", tM, t_xq_tp)
            S.op("vector", lambda e, t=t: e.tensor_scalar(
                xrl[:, 1024 * (t % 2):1024 * (t % 2) + 1024],
                xrl[:, 1024 * (t % 2):1024 * (t % 2) + 1024],
                mean_b[:, t:t + 1], c2_b[:, t:t + 1], ALU.subtract, ALU.mult), vV, 1)
            S.op("vector", lambda e, t=t: e.tensor_scalar(
                xrl[:, 1024 * (t % 2):1024 * (t % 2) + 1024],
                xrl[:, 1024 * (t % 2):1024 * (t % 2) + 1024],
                MAGIC, -MAGIC, ALU.add, ALU.add), vV, 1)
            S.op("vector", lambda e, t=t: e.tensor_scalar(
                xq_t[:, :], xrl[:, 1024 * (t % 2):1024 * (t % 2) + 1024],
                -128.0, 127.0, ALU.max, ALU.min), vV, 1)
            v_rq = S.cnt(vV)
            S.wait("tensor", vV, v_rq)
            if v_xq_ev:
                S.wait("tensor", vV, v_xq_ev)
            for dc in range(DC):
                S.op("tensor", lambda e, dc=dc: e.transpose(
                    B(dc, 384, 128), xq_t[:, 128 * dc:128 * dc + 128],
                    identf[:, :]), tM, 1)
            t_xq_tp = S.cnt(tM)
            S.wait("vector", tM, t_xq_tp)
            S.op("vector", lambda e, t=t: e.tensor_copy(
                bass.AP(xq_fm[:, :].tensor, 128 * t, [[DC * T, 128], [T, 8], [1, 128]]),
                bass.AP(ps[:, :].tensor, 384, [[4096, 128], [512, 8], [1, 128]])), vV, 1)
            v_xq_ev = S.cnt(vV)
        v_xq_done = S.cnt(vV)
        if "xq" in dbgd:
            S.wait("sync", vV, v_xq_done)
            S.op("sync", lambda e: e.dma_start(out=dbgd["xq"], in_=xq_fm[:, :]), dS, 16)
            for eng in ("vector", "scalar", "tensor", "gpsimd"):
                S.wait(eng, dS)

        ckpt(51)
        for hc in range(DC):
            if hc >= 1:
                S.wait("gpsimd", vV)
            S.op("gpsimd", lambda e, hc=hc: e.dma_start(
                out=wdnat[:, :], in_=wdd[128 * hc:128 * hc + 128, :]), dG, 16)
            d_wd = S.cnt(dG)
            S.wait("vector", dG, d_wd)
            S.op("vector", lambda e, hc=hc: e.tensor_reduce(
                smaxd[:, hc:hc + 1], wdnat[:, :], AX.X, ALU.max,
                apply_absolute_value=True), vV, 1)
            vfill()
            S.op("vector", lambda e, hc=hc: e.reciprocal(rcp1[:, 0:1], smaxd[:, hc:hc + 1]), vV, 1)
            vfill()
            S.op("vector", lambda e: e.tensor_scalar(
                rcp1[:, 0:1], rcp1[:, 0:1], 127.0, None, ALU.mult), vV, 1)
            vfill()
            S.op("vector", lambda e: e.tensor_scalar(
                wdnat[:, :], wdnat[:, :], rcp1[:, 0:1], MAGIC, ALU.mult, ALU.add), vV, 1)
            S.op("vector", lambda e: e.tensor_scalar(
                wdi[:, :], wdnat[:, :], -MAGIC, None, ALU.add), vV, 1)
            v_wd = S.cnt(vV)
            S.wait("gpsimd", vV, v_wd)
            S.op("gpsimd", lambda e, hc=hc: e.dma_start(
                out=scr_d.ap()[128 * hc:128 * hc + 128, :], in_=wdi[:, :]), dG, 16)
        dG_wd_scr = S.cnt(dG)

        S.op("vector", lambda e: e.tensor_scalar(
            ag_b[:, :], smaxg[:, :], smb[:, 0:1], 1.0 / 127.0, ALU.mult, ALU.mult), vV, 1)
        S.op("vector", lambda e: e.tensor_scalar(
            au_b[:, :], smaxu[:, :], smb[:, 0:1], 1.0 / 127.0, ALU.mult, ALU.mult), vV, 1)
        vfill()
        S.op("vector", lambda e: e.tensor_mul(agau[:, :], ag_b[:, :], au_b[:, :]), vV, 1)
        v_agau = S.cnt(vV)

        ckpt(61)
        S.wait("tensor", vV, v_xq_done)
        ps_free_to(psA)
        psB = len(ps_stack)
        ps_gp = (palloc("ps_g0", [128, 1024], F32), palloc("ps_g1", [128, 1024], F32))
        ps_up = (palloc("ps_u0", [128, 1024], F32), palloc("ps_u1", [128, 1024], F32))

        v_evac = {}
        s_sigm = {}
        dG_usp = {}
        pend_merge = None
        gustop = 2 * NIC if (stop_phase is None or stop_phase < 300) else (stop_phase - 300)
        for step in range(min(2 * NIC, gustop)):
            ic, th = step // 2, step % 2
            pp = step % 2
            pg, pu = ps_gp[pp], ps_up[pp]
            if step >= 2:
                S.wait("tensor", vV, v_evac[step - 2])
                S.wait("tensor", sS, s_sigm[step - 2])
            for hc in range(DC):
                if step == 0:
                    S.wait("tensor", dS, wT_hc[hc])
                for t4 in range(2):
                    S.op("tensor", lambda e, ic=ic, th=th, hc=hc, t4=t4, pg=pg: e.matmul(
                        pg[:, 512 * t4:512 * t4 + 512],
                        wgT_s[:, IFF * hc + 128 * ic:IFF * hc + 128 * ic + 128],
                        xq_fm[:, T * hc + 1024 * th + 512 * t4:T * hc + 1024 * th + 512 * t4 + 512],
                        start=(hc == 0), stop=(hc == DC - 1)), tM, 1)
            for hc in range(DC):
                for t4 in range(2):
                    S.op("tensor", lambda e, ic=ic, th=th, hc=hc, t4=t4, pu=pu: e.matmul(
                        pu[:, 512 * t4:512 * t4 + 512],
                        wuT_s[:, IFF * hc + 128 * ic:IFF * hc + 128 * ic + 128],
                        xq_fm[:, T * hc + 1024 * th + 512 * t4:T * hc + 1024 * th + 512 * t4 + 512],
                        start=(hc == 0), stop=(hc == DC - 1)), tM, 1)
            t_gu = S.cnt(tM)
            S.wait("scalar", tM, t_gu)
            if step == 0:
                S.wait("scalar", vV, v_agau)
            S.op("scalar", lambda e, pp=pp, pg=pg: e.activation(
                g_sb[:, 1024 * pp:1024 * pp + 1024], pg[:, :], AF.Identity), sS, 1)
            s_gev = S.cnt(sS)
            S.op("scalar", lambda e, ic=ic, pp=pp: e.activation(
                sig_b[:, 1024 * pp:1024 * pp + 1024],
                g_sb[:, 1024 * pp:1024 * pp + 1024], AF.Sigmoid,
                scale=ag_b[:, ic:ic + 1]), sS, 1)
            s_sigm[step] = S.cnt(sS)
            S.wait("vector", tM, t_gu)
            if step >= 4:
                S.wait("vector", dG, dG_usp[step - 4])
            uo = 1024 * (step % 4)
            S.op("vector", lambda e, uo=uo, pu=pu: e.tensor_copy(
                uf[:, uo:uo + 1024], pu[:, :]), vV, 1)
            if pend_merge is not None:
                S.op("vector", lambda e, ic_=pend_merge: e.tensor_tensor(
                    wmax[:, ic_:ic_ + 1], wmax[:, ic_:ic_ + 1], rcp1[:, 0:1],
                    ALU.max), vV, 1)
                pend_merge = None
            S.wait("vector", sS, s_gev)
            S.op("vector", lambda e, uo=uo, pp=pp: e.tensor_mul(
                uf[:, uo:uo + 1024], g_sb[:, 1024 * pp:1024 * pp + 1024],
                uf[:, uo:uo + 1024]), vV, 1)
            S.wait("vector", sS, s_sigm[step])
            S.op("vector", lambda e, uo=uo, pp=pp: e.tensor_mul(
                uf[:, uo:uo + 1024], uf[:, uo:uo + 1024],
                sig_b[:, 1024 * pp:1024 * pp + 1024]), vV, 1)
            S.op("vector", lambda e, ic=ic, th=th, uo=uo: e.tensor_reduce(
                wmax[:, ic:ic + 1] if th == 0 else rcp1[:, 0:1],
                uf[:, uo:uo + 1024], AX.X, ALU.max, apply_absolute_value=True), vV, 1)
            if th == 1:
                pend_merge = ic
            v_evac[step] = S.cnt(vV)
            S.wait("gpsimd", vV, v_evac[step])
            S.op("gpsimd", lambda e, ic=ic, th=th, uo=uo: e.dma_start(
                out=u_dram.ap()[128 * ic:128 * ic + 128, 1024 * th:1024 * th + 1024],
                in_=uf[:, uo:uo + 1024]), dG, 16)
            dG_usp[step] = S.cnt(dG)
        t_gu_all = S.cnt(tM)
        if pend_merge is not None:
            vfill()
            S.op("vector", lambda e, ic_=pend_merge: e.tensor_tensor(
                wmax[:, ic_:ic_ + 1], wmax[:, ic_:ic_ + 1], rcp1[:, 0:1],
                ALU.max), vV, 1)
            pend_merge = None
            vfill()
        if stop_phase is not None and 300 <= stop_phase < 400:
            ckpt(299)
        if "wmax" in dbgd:
            S.wait("sync", vV)
            S.op("sync", lambda e: e.dma_start(out=dbgd["wmax"], in_=wmax[:, :]), dS, 16)
            for eng in ("vector", "scalar", "tensor", "gpsimd"):
                S.wait(eng, dS)

        ckpt(71)
        S.op("vector", lambda e: e.tensor_mul(ccb[:, :], wmax[:, :], agau[:, :]), vV, 1)
        vfill()
        S.op("vector", lambda e: e.tensor_reduce(red1[:, :], ccb[:, :], AX.X, ALU.max), vV, 1)
        v_um = S.cnt(vV)
        S.wait("tensor", vV, v_um)
        S.op("tensor", lambda e: e.transpose(ps_up[1][0:1, 0:128], red1[:, :],
                                             identf[:, :]), tM, 1)
        t_um = S.cnt(tM)
        S.wait("vector", tM, t_um)
        S.op("vector", lambda e: e.tensor_reduce(redr[0:1, 0:1], ps_up[1][0:1, 0:128],
                                                 AX.X, ALU.max), vV, 1)
        v_um2 = S.cnt(vV)
        S.wait("gpsimd", vV, v_um2)
        S.op("gpsimd", lambda e: e.dma_start(out=cc2i.ap(), in_=redr[0:1, 0:1]), dG, 16)
        S.wait("gpsimd", dG)
        S.op("gpsimd", lambda e: e.collective_compute(
            "AllReduce", ALU.max, replica_groups=[list(range(NCORE))],
            ins=[cc2i.ap()], outs=[cc2o.ap()]), cC, 1)
        S.wait("gpsimd", cC)
        S.op("gpsimd", lambda e: e.dma_start(out=sm1[0:1, 4:5], in_=cc2o.ap()), dG, 16)
        dG_g2 = S.cnt(dG)

        S.op("scalar", lambda e: e.activation(sm1[0:1, 5:6], clp[0:1, 1:2], AF.Sigmoid), sS, 1)
        s_sig2 = S.cnt(sS)
        S.wait("vector", dG, dG_g2)
        S.wait("vector", sS, s_sig2)
        S.op("vector", lambda e: e.scalar_tensor_tensor(
            sm1[0:1, 6:7], sm1[0:1, 4:5], 1.0 / 127.0, sm1[0:1, 5:6],
            ALU.mult, ALU.mult), vV, 1)
        vfill()
        S.op("vector", lambda e: e.reciprocal(sm1[0:1, 7:8], sm1[0:1, 6:7]), vV, 1)
        v_su = S.cnt(vV)
        S.wait("tensor", vV, v_su)
        S.wait("tensor", vV, v_evac[2 * NIC - 1])
        S.op("tensor", lambda e: e.matmul(ps_up[0][:, 0:2], onesr[:, :], sm1[0:1, 6:8],
                                          start=True, stop=True), tM, 1)
        t_smb2 = S.cnt(tM)
        S.wait("vector", tM, t_smb2)
        S.op("vector", lambda e: e.tensor_copy(smb[:, 2:4], ps_up[0][:, 0:2]), vV, 1)
        vfill()
        S.op("vector", lambda e: e.tensor_scalar(
            ccb[:, :], agau[:, :], smb[:, 3:4], None, ALU.mult), vV, 1)
        S.op("vector", lambda e: e.tensor_scalar(
            sdsu[:, :], smaxd[:, :], smb[:, 2:3], 1.0 / 127.0, ALU.mult, ALU.mult), vV, 1)
        v_ffs = S.cnt(vV)

        ckpt(81)
        sb_free_to(sbB3)

        sbB4 = len(sb_stack)
        h_bf = alloc("h_bf", [128, NIC * T], BF16)
        wdT_s = alloc("wdT_s", [128, NIC * 1024], BF16)
        url = alloc("url", [128, 2 * 2048], F32)
        s_rl = alloc("s_rl", [128, 4 * T], BF16)
        sg_rot = alloc("sg_rot", [128, 2 * T], F32)
        sig2 = alloc("sig2", [128, 2 * 2048], F32)

        S.wait("sync", tM, t_gu_all)
        S.wait("sync", vV, v_ffs)
        S.wait("gpsimd", tM, t_gu_all)
        S.wait("gpsimd", vV, v_ffs)

        S.wait("sync", dG, dG_wd_scr)
        for ic in range(NIC):
            S.op("sync", lambda e, ic=ic: e.dma_start(
                out=wdT_s[:, 1024 * ic:1024 * ic + 1024],
                in_=scr_d.ap()[:, 128 * ic:128 * ic + 128], transpose=True), dS, 16)
        dS_wdT = S.cnt(dS)

        v_rnd = {}
        for ic in range(NIC):
            sl = ic % 2
            eng, sem = ("gpsimd", dG) if sl == 0 else ("sync", dS)
            if sl == 1:
                S.wait("sync", dG, dG_usp[2 * ic + 1])
            if ic >= 2:
                S.wait(eng, vV, v_rnd[ic - 2])
            S.op(eng, lambda e, ic=ic, sl=sl: e.dma_start(
                out=url[:, 2048 * sl:2048 * sl + 2048],
                in_=u_dram.ap()[128 * ic:128 * ic + 128, :]), sem, 16)
            d_u = S.cnt(sem)
            S.wait("vector", sem, d_u)
            S.wait("vector", vV, v_ffs)
            S.op("vector", lambda e, ic=ic, sl=sl: e.tensor_scalar(
                url[:, 2048 * sl:2048 * sl + 2048], url[:, 2048 * sl:2048 * sl + 2048],
                ccb[:, ic:ic + 1], MAGIC, ALU.mult, ALU.add), vV, 1)
            S.op("vector", lambda e, sl=sl: e.tensor_scalar(
                url[:, 2048 * sl:2048 * sl + 2048], url[:, 2048 * sl:2048 * sl + 2048],
                -MAGIC, -128.0, ALU.add, ALU.max), vV, 1)
            S.op("vector", lambda e, ic=ic, sl=sl: e.tensor_scalar(
                h_bf[:, T * ic:T * ic + T], url[:, 2048 * sl:2048 * sl + 2048],
                127.0, None, ALU.min), vV, 1)
            v_rnd[ic] = S.cnt(vV)
        v_rnd_all = S.cnt(vV)
        if "h" in dbgd:
            S.wait("sync", vV, v_rnd_all)
            S.op("sync", lambda e: e.dma_start(out=dbgd["h"], in_=h_bf[:, :]), dS, 16)
            for eng in ("vector", "scalar", "tensor", "gpsimd"):
                S.wait(eng, dS)

        if "scrd" in dbgd:
            S.wait("sync", dG, dG_wd_scr)
            S.op("sync", lambda e: e.dma_start(out=dbgd["scrd"], in_=scr_d.ap()), dS, 16)
            for eng in ("vector", "scalar", "tensor", "gpsimd"):
                S.wait(eng, dS)
        if "wdt" in dbgd:
            S.wait("sync", dS, dS_wdT)
            S.op("sync", lambda e: e.dma_start(out=dbgd["wdt"], in_=wdT_s[:, :]), dS, 16)
            for eng in ("vector", "scalar", "tensor", "gpsimd"):
                S.wait(eng, dS)

        ckpt(91)
        ps_free_to(psB)
        psC = len(ps_stack)
        ps_d0 = palloc("ps_d0", [128, 2048], F32)
        ps_d1 = palloc("ps_d1", [128, 2048], F32)
        ps_dx = (ps_d0, ps_d1)

        S.wait("tensor", dS, dS_wdT)
        S.wait("tensor", vV, v_rnd_all)
        v_sg = {}
        s_sg = {}
        dS_sgsp = {}
        d_s = {}

        def srl_load(hc):
            sl4 = hc % 4
            if hc >= 4:
                S.wait("sync", vV, v_sg[hc - 4])
            S.op("sync", lambda e, hc=hc, sl4=sl4: e.dma_start(
                out=s_rl[:, T * sl4:T * sl4 + T],
                in_=s_dram.ap()[:, T * hc:T * hc + T]), dS, 16)
            d_s[hc] = S.cnt(dS)

        srl_load(0)
        srl_load(1)
        for hc in range(DC):
            pp = ps_dx[hc % 2]
            if hc >= 2:
                S.wait("tensor", vV, v_sg[hc - 2])
                S.wait("tensor", sS, s_sg[hc - 2])
            for ic in range(NIC):
                for t4 in range(4):
                    S.op("tensor", lambda e, hc=hc, ic=ic, t4=t4, pp=pp: e.matmul(
                        pp[:, 512 * t4:512 * t4 + 512],
                        wdT_s[:, 1024 * ic + 128 * hc:1024 * ic + 128 * hc + 128],
                        h_bf[:, T * ic + 512 * t4:T * ic + 512 * t4 + 512],
                        start=(ic == 0), stop=(ic == NIC - 1)), tM, 1)
            t_d = S.cnt(tM)
            sl = hc % 2
            if hc + 2 < DC:
                srl_load(hc + 2)
            S.wait("scalar", tM, t_d)
            S.op("scalar", lambda e, hc=hc, sl=sl, pp=pp: e.activation(
                sig2[:, 2048 * sl:2048 * sl + 2048], pp[:, :], AF.Sigmoid,
                scale=sdsu[:, hc:hc + 1]), sS, 1)
            s_sg[hc] = S.cnt(sS)
            S.wait("vector", sS, s_sg[hc])
            S.wait("vector", dS, d_s[hc])
            if hc >= 2:
                S.wait("vector", dS, dS_sgsp.get(hc - 2, 0))
            S.op("vector", lambda e, hc=hc, sl=sl, pp=pp: e.scalar_tensor_tensor(
                sig2[:, 2048 * sl:2048 * sl + 2048], pp[:, :], sdsu[:, hc:hc + 1],
                sig2[:, 2048 * sl:2048 * sl + 2048], ALU.mult, ALU.mult), vV, 1)
            S.op("vector", lambda e, hc=hc, sl=sl: e.tensor_mul(
                sg_rot[:, T * sl:T * sl + T], sig2[:, 2048 * sl:2048 * sl + 2048],
                s_rl[:, T * (hc % 4):T * (hc % 4) + T]), vV, 1)
            v_sg[hc] = S.cnt(vV)
            S.wait("sync", vV, v_sg[hc])
            S.op("sync", lambda e, hc=hc, sl=sl: e.dma_start(
                out=sg_dram.ap()[:, T * hc:T * hc + T],
                in_=sg_rot[:, T * sl:T * sl + T]), dS, 16)
            dS_sgsp[hc] = S.cnt(dS)
        v_sg_all = S.cnt(vV)
        dS_sg_all = S.cnt(dS)
        if "sg" in dbgd:
            S.wait("sync", dS, dS_sg_all)
            S.op("sync", lambda e: e.dma_start(out=dbgd["sg"], in_=sg_dram.ap()), dS, 16)
            for eng in ("vector", "scalar", "tensor", "gpsimd"):
                S.wait(eng, dS)
        if "s" in dbgd:
            S.wait("sync", dS, dS_sg_all)
            S.op("sync", lambda e: e.dma_start(out=dbgd["s"], in_=s_dram.ap()), dS, 16)
            for eng in ("vector", "scalar", "tensor", "gpsimd"):
                S.wait(eng, dS)
        if "sm1" in dbgd:
            S.wait("sync", vV, v_sg_all)
            S.op("sync", lambda e: e.dma_start(out=dbgd["sm1"], in_=sm1[:, :]), dS, 16)
            for eng in ("vector", "scalar", "tensor", "gpsimd"):
                S.wait(eng, dS)

        for _pi in range(pad_sync):
            if pad_kind == "dma":
                S.op("sync", lambda e: e.dma_start(out=u_dram.ap()[0:1, 0:16],
                                                   in_=u_dram.ap()[2:3, 0:16]), dS, 16)
            else:
                S.wait("sync", vV, 1)
        if pad_sync:
            S.wait("sync", dS)
        ckpt(95)
        sb_free_to(sbB4)
        ps_free_to(psC)
        ps_f = palloc("ps_f", [128, 4096], F32)
        sgt = alloc("sgt", [128, 4 * 1024], F32)
        xfin = alloc("xfin", [128, 4 * 1024], F32)
        outb = alloc("outb", [128, 4 * 1024], F32)

        S.wait("tensor", vV, v_sg_all)
        S.wait("sync", dS, dS_sg_all)
        S.wait("gpsimd", dS, dS_sg_all)
        dS_out = {}
        dG_sgt = {}
        t_f_hist = {}
        v_add = 0
        tstop = 16 if (stop_phase is None or stop_phase < 101) else (stop_phase - 101)
        for t in range(min(16, tstop)):
            sl = t % 4
            if t >= 4:
                S.wait("sync", dS, dS_out[t - 4])
                S.wait("gpsimd", tM, t_f_hist[t - 4])
            if "sgt" not in dskip:
                for cg in range(DC):
                    S.op("gpsimd", lambda e, t=t, sl=sl, cg=cg: e.dma_start(
                        out=sgt[:, 1024 * sl + 128 * cg:1024 * sl + 128 * cg + 128],
                        in_=sg_dram.ap()[:, T * cg + 128 * t:T * cg + 128 * t + 128]), dG, 16)
            dG_sgt[t] = S.cnt(dG)
            if "xfin" in dskip:
                S.op("sync", lambda e, t=t, sl=sl: e.dma_start(
                    out=xfin[:, 1024 * sl:1024 * sl + 1024],
                    in_=u_dram.ap()[0:128, 0:1024]), dS, 16)
            else:
                S.op("sync", lambda e, t=t, sl=sl: e.dma_start(
                    out=xfin[:, 1024 * sl:1024 * sl + 1024],
                    in_=xd[2 + 128 * t:2 + 128 * t + 128, :]), dS, 16)
            d_xf = S.cnt(dS)
            if v_add:
                S.wait("tensor", vV, v_add)
            S.wait("tensor", dG, dG_sgt[t])
            if "tp" not in dskip:
                for cg in range(DC):
                    S.op("tensor", lambda e, sl=sl, cg=cg: e.transpose(
                        ps_f[:, 512 * cg + 384:512 * cg + 384 + 128],
                        sgt[:, 1024 * sl + 128 * cg:1024 * sl + 128 * cg + 128],
                        identf[:, :]), tM, 1)
            t_f = S.cnt(tM)
            t_f_hist[t] = t_f
            S.wait("vector", tM, t_f)
            S.wait("vector", dS, d_xf)
            for hb in range(2):
                if "add" in dskip:
                    S.op("vector", lambda e, sl=sl, hb=hb: e.memset(
                        outb[:, 1024 * sl + 512 * hb:1024 * sl + 512 * hb + 512], 0.5), vV, 1)
                else:
                    S.op("vector", lambda e, sl=sl, hb=hb: e.tensor_add(
                        bass.AP(outb[:, :].tensor, 1024 * sl + 512 * hb, [[4096, 128], [128, 4], [1, 128]]),
                        bass.AP(ps_f[:, :].tensor, 2048 * hb + 384, [[4096, 128], [512, 4], [1, 128]]),
                        bass.AP(xfin[:, :].tensor, 1024 * sl + 512 * hb, [[4096, 128], [128, 4], [1, 128]])), vV, 1)
            v_add = S.cnt(vV)
            S.wait("sync", vV, v_add)
            if "out" in dskip:
                S.op("sync", lambda e, t=t, sl=sl: e.dma_start(
                    out=u_dram.ap()[128:256, 0:1024],
                    in_=outb[:, 1024 * sl:1024 * sl + 1024]), dS, 16)
            else:
                S.op("sync", lambda e, t=t, sl=sl: e.dma_start(
                    out=outd[128 * t:128 * t + 128, :],
                    in_=outb[:, 1024 * sl:1024 * sl + 1024]), dS, 16)
            dS_out[t] = S.cnt(dS)
        if stop_phase is not None and stop_phase >= 101:
            ckpt(200)
        S.wait("sync", dS)

    except _StopBuild:
        pass
    sb_free_to(0)
    ps_free_to(0)

    with nc.Block() as block:

        @block.sync
        def _(sync):
            S.emit("sync", sync)

        @block.tensor
        def _(tensor):
            S.emit("tensor", tensor)

        @block.vector
        def _(vector):
            S.emit("vector", vector)

        @block.scalar
        def _(scalar):
            S.emit("scalar", scalar)

        @block.gpsimd
        def _(gpsimd):
            S.emit("gpsimd", gpsimd)

    for cm in reversed(sems_cm):
        cm.__exit__(None, None, None)
    return nc


_NC_CACHE = {}


def _get_nc(dbg=()):
    key = tuple(sorted(dbg))
    if key not in _NC_CACHE:
        _NC_CACHE[key] = build_nc(dbg=key)
    return _NC_CACHE[key]


def make_inputs(inputs, dbg=()):
    x = np.asarray(inputs["L0_attn_x"], np.float32)
    wq = np.asarray(inputs["L0_attn_Wq"], np.float32)
    bq = np.asarray(inputs["L0_attn_bq"], np.float32)
    wk = np.asarray(inputs["L0_attn_Wk"], np.float32)
    bk = np.asarray(inputs["L0_attn_bk"], np.float32)
    wv = np.asarray(inputs["L0_attn_Wv"], np.float32)
    bv = np.asarray(inputs["L0_attn_bv"], np.float32)
    wo = np.asarray(inputs["L0_attn_Wo"], np.float32)
    bo = np.asarray(inputs["L0_attn_bo"], np.float32)
    cx = np.asarray(inputs["L0_ffn_clip_x"], np.float32)
    cd = np.asarray(inputs["L0_ffn_clip_down"], np.float32)
    wg = np.ascontiguousarray(np.asarray(inputs["L0_ffn_w_gate"], np.float32))
    wu = np.ascontiguousarray(np.asarray(inputs["L0_ffn_w_up"], np.float32))
    wd = np.ascontiguousarray(np.asarray(inputs["L0_ffn_w_down"], np.float32))

    shared = {
        "wqT": np.ascontiguousarray(wq.T), "wkT": np.ascontiguousarray(wk.T),
        "wvT": np.ascontiguousarray(wv.T), "woT": np.ascontiguousarray(wo.T),
        "bqf": np.ascontiguousarray(bq.reshape(8, 128).T),
        "bkf": np.ascontiguousarray(bk.reshape(8, 128).T),
        "bof": np.ascontiguousarray(bo.reshape(8, 128).T),
        "bvr": bv.reshape(1, D),
        "wg": wg, "wu": wu, "wd": wd,
        "clips": np.array([[cx[0], cd[0]]], np.float32),
        "ident": np.eye(128, dtype=BF16NP),
        "identf": np.eye(128, dtype=np.float32),
        "onesr": np.ones((1, 128), np.float32),
    }
    masks = [_mask_host(0), _mask_host(1)]
    in_maps = []
    for c in range(NCORE):
        b, hf = c // 2, c % 2
        lo, hi = hf * T - 2, hf * T + T + 2
        sl = x[b, max(lo, 0):min(hi, 4096)]
        if lo < 0:
            sl = np.concatenate([np.zeros((-lo, D), np.float32), sl], 0)
        if hi > 4096:
            sl = np.concatenate([sl, np.zeros((hi - 4096, D), np.float32)], 0)
        m = dict(shared)
        m["x"] = np.ascontiguousarray(sl)
        m["mask"] = masks[hf]
        in_maps.append(m)
    return in_maps


def kernel(**inputs):
    nc = _get_nc()
    in_maps = make_inputs(inputs)
    res = run_bass_kernel_spmd(nc, in_maps, core_ids=list(range(NCORE)))
    out = np.empty((4, 4096, D), np.float32)
    for c in range(NCORE):
        b, hf = c // 2, c % 2
        out[b, hf * T:hf * T + T] = res.results[c]["out"]
    return out


# version 09
# speedup vs baseline: 1.0107x; 1.0107x over previous
import os
import numpy as np
import ml_dtypes
import concourse.bass as bass
import concourse.mybir as mybir
from concourse.bass_utils import run_bass_kernel_spmd

F32 = mybir.dt.float32
BF16 = mybir.dt.bfloat16
AF = mybir.ActivationFunctionType
ALU = mybir.AluOpType
AX = mybir.AxisListType

MAGIC = 12582912.0
ESHIFT = -8.0
SUMCLAMP = 2.0 ** -40

D = 1024
DC = 8
T = 2048
TH = 2052
QT = 124
NJ = 17
QLAST = T - QT * (NJ - 1)
IFF = 2816
NIC = 22
NCORE = 8
BF16NP = ml_dtypes.bfloat16


def _mask_host(parity):
    m = np.zeros((NJ, 128, QT), np.float32)
    r = np.arange(128)[:, None]
    qc = np.arange(QT)[None, :]
    for j in range(NJ):
        qn = QT if j < NJ - 1 else QLAST
        band = (r - qc >= 0) & (r - qc <= 4) & (qc < qn)
        gk = parity * T - 2 + QT * j + r + 0 * qc
        valid = (gk >= 0) & (gk < 4096)
        m[j] = (band & valid).astype(np.float32)
    return np.ascontiguousarray(m.transpose(1, 0, 2).reshape(128, NJ * QT)).astype(BF16NP)


class Sched:

    ENGS = ("sync", "tensor", "vector", "scalar", "gpsimd")
    SPARSE = ("tM", "vV", "sS")
    SPARSE_ON = True

    def __init__(self):
        self.q = {e: [] for e in self.ENGS}
        self.val = {}
        self.waited = {}

    def op(self, eng, fn, sem=None, n=1):
        if sem is not None:
            self.val[sem] = self.val.get(sem, 0) + n
            self.q[eng].append(("op", fn, sem, n, self.val[sem]))
        else:
            self.q[eng].append(("op", fn, None, 0, 0))

    def wait(self, eng, sem, v=None):
        v = self.val.get(sem, 0) if v is None else v
        if v > 0:
            self.waited.setdefault(sem.name if hasattr(sem, "name") else id(sem), set()).add(v)
            self.q[eng].append(("wait", sem, v))

    def cnt(self, sem):
        return self.val.get(sem, 0)

    def emit(self, eng, engobj):
        trace = os.environ.get("SCHED_TRACE") == eng
        last_cum = {}
        for ent in self.q[eng]:
            if ent[0] == "op" and ent[2] is not None:
                last_cum[id(ent[2])] = ent[4]
        pending = {}
        for i, ent in enumerate(self.q[eng]):
            if ent[0] == "wait":
                _, sem, v = ent
                engobj.wait_ge(sem, v)
                if trace:
                    print(f"[{eng} {i}] WAIT {getattr(sem, 'name', '?')} >= {v}")
                continue
            _, fn, sem, n, cum = ent
            inst = fn(engobj)
            if sem is None:
                if trace:
                    print(f"[{eng} {i}] op (nosem)")
                continue
            key = sem.name if hasattr(sem, "name") else id(sem)
            sparse = self.SPARSE_ON and getattr(sem, "name", "") in self.SPARSE
            if not sparse:
                inst.then_inc(sem, n)
                if trace:
                    print(f"[{eng} {i}] op inc {key} +{n} (dense, cum={cum})")
                continue
            pend = pending.get(id(sem), 0) + n
            wset = self.waited.get(key, set())
            if cum in wset or cum == last_cum[id(sem)]:
                inst.then_inc(sem, pend)
                if trace:
                    print(f"[{eng} {i}] op inc {key} +{pend} (cum={cum})")
                pend = 0
            else:
                if trace:
                    print(f"[{eng} {i}] op ---- {key} (cum={cum})")
            pending[id(sem)] = pend


class _StopBuild(Exception):
    pass


def build_nc(dbg=(), races=False, stop_phase=None, dskip=(), pad_sync=0, pad_kind="dma"):
    nc = bass.Bass("TRN2", target_bir_lowering=False,
                   detect_race_conditions=races)

    xd = nc.dram_tensor("x", [TH, D], F32, kind="ExternalInput").ap()
    wqT = nc.dram_tensor("wqT", [D, D], F32, kind="ExternalInput").ap()
    wkT = nc.dram_tensor("wkT", [D, D], F32, kind="ExternalInput").ap()
    wvT = nc.dram_tensor("wvT", [D, D], F32, kind="ExternalInput").ap()
    woT = nc.dram_tensor("woT", [D, D], F32, kind="ExternalInput").ap()
    bqf = nc.dram_tensor("bqf", [128, 8], F32, kind="ExternalInput").ap()
    bkf = nc.dram_tensor("bkf", [128, 8], F32, kind="ExternalInput").ap()
    bof = nc.dram_tensor("bof", [128, 8], F32, kind="ExternalInput").ap()
    bvr = nc.dram_tensor("bvr", [1, D], F32, kind="ExternalInput").ap()
    wgd = nc.dram_tensor("wg", [IFF, D], F32, kind="ExternalInput").ap()
    wud = nc.dram_tensor("wu", [IFF, D], F32, kind="ExternalInput").ap()
    wdd = nc.dram_tensor("wd", [D, IFF], F32, kind="ExternalInput").ap()
    clips = nc.dram_tensor("clips", [1, 2], F32, kind="ExternalInput").ap()
    identd = nc.dram_tensor("ident", [128, 128], BF16, kind="ExternalInput").ap()
    identfd = nc.dram_tensor("identf", [128, 128], F32, kind="ExternalInput").ap()
    onesd = nc.dram_tensor("onesr", [1, 128], F32, kind="ExternalInput").ap()
    maskd = nc.dram_tensor("mask", [128, NJ * QT], BF16, kind="ExternalInput").ap()
    outd = nc.dram_tensor("out", [T, D], F32, kind="ExternalOutput").ap()

    u_dram = nc.dram_tensor("u_scr", [IFF, T], F32)
    s_dram = nc.dram_tensor("s_scr", [128, DC * T], BF16)
    sg_dram = nc.dram_tensor("sg_scr", [128, DC * T], F32)
    scr_g = nc.dram_tensor("scr_g", [IFF, D], BF16)
    scr_u = nc.dram_tensor("scr_u", [IFF, D], BF16)
    scr_d = nc.dram_tensor("scr_d", [D, IFF], BF16)
    cc1i = nc.dram_tensor("cc1i", [1, 1], F32)
    cc1o = nc.dram_tensor("cc1o", [1, 1], F32)
    cc2i = nc.dram_tensor("cc2i", [1, 1], F32)
    cc2o = nc.dram_tensor("cc2o", [1, 1], F32)

    dbgd = {}
    for k in dbg:
        shape, dt = {
            "xn16": ([128, DC * TH], BF16), "q": ([128, 4 * T], BF16),
            "k": ([128, 4 * TH], BF16), "v": ([128, NJ * 520], BF16),
            "attnfm": ([128, DC * T], BF16), "xq": ([128, DC * T], BF16),
            "h": ([128, NIC * T], BF16), "sg": ([128, DC * T], F32),
            "sm1": ([1, 8], F32), "wmax": ([128, NIC], F32),
            "stats": ([128, 7 * 17], F32), "xtm": ([128, 17 * 1024], F32),
            "s": ([128, DC * T], BF16), "wdt": ([128, NIC * 1024], BF16),
            "scrd": ([D, IFF], BF16),
        }[k]
        dbgd[k] = nc.dram_tensor("dbg_" + k, shape, dt, kind="ExternalOutput").ap()

    S = Sched()
    sb_stack, ps_stack = [], []

    def alloc(name, shape, dtype):
        cm = nc.sbuf_tensor(name, shape, dtype)
        h = cm.__enter__()
        sb_stack.append(cm)
        return h

    def sb_free_to(mark):
        while len(sb_stack) > mark:
            sb_stack.pop().__exit__(None, None, None)

    def palloc(name, shape, dtype):
        cm = nc.psum_tensor(name, shape, dtype)
        h = cm.__enter__()
        ps_stack.append(cm)
        return h

    def ps_free_to(mark):
        while len(ps_stack) > mark:
            ps_stack.pop().__exit__(None, None, None)

    sems_cm = [nc.semaphore(n) for n in ("dS", "dG", "tM", "vV", "sS", "cC")]
    dS, dG, tM, vV, sS, cC = [cm.__enter__() for cm in sems_cm]

    def ckpt(n):
        fire = stop_phase is not None and n >= stop_phase
        if n == 299 and stop_phase is not None and 300 <= stop_phase < 400:
            fire = True
        if fire:
            for eng in ("sync", "gpsimd"):
                S.wait(eng, tM)
                S.wait(eng, vV)
                S.wait(eng, sS)
            S.wait("sync", dG)
            S.wait("sync", dS)
            S.op("sync", lambda e: e.dma_start(out=outd, in_=u_dram.ap()[0:1024, :]), dS, 16)
            S.wait("sync", dS)
            raise _StopBuild()

    try:
        ident = alloc("ident_s", [128, 128], BF16)
        identf = alloc("identf_s", [128, 128], F32)
        onesr = alloc("ones_s", [1, 128], F32)
        mask = alloc("mask_s", [128, NJ * QT], BF16)
        bq_s = alloc("bq_s", [128, 8], F32)
        bk_s = alloc("bk_s", [128, 8], F32)
        bo_s = alloc("bo_s", [128, 8], F32)
        bvb = alloc("bvb_s", [128, D], F32)
        clp = alloc("clp_s", [1, 2], F32)
        mean_b = alloc("mean_s", [128, 17], F32)
        rstd_b = alloc("rstd_s", [128, 17], F32)
        sum_b = alloc("sumb_s", [128, 17], F32)
        msq_b = alloc("msqb_s", [128, 17], F32)
        amax_b = alloc("amax_s", [128, 17], F32)
        c2_b = alloc("c2_s", [128, 17], F32)
        smaxg = alloc("smaxg_s", [128, NIC], F32)
        smaxu = alloc("smaxu_s", [128, NIC], F32)
        smaxd = alloc("smaxd_s", [128, DC], F32)
        ag_b = alloc("ag_s", [128, NIC], F32)
        au_b = alloc("au_s", [128, NIC], F32)
        agau = alloc("agau_s", [128, NIC], F32)
        ccb = alloc("ccb_s", [128, NIC], F32)
        sdsu = alloc("sdsu_s", [128, DC], F32)
        wmax = alloc("wmax_s", [128, NIC], F32)
        sm1 = alloc("sm1_s", [1, 8], F32)
        smb = alloc("smb_s", [128, 4], F32)
        red1 = alloc("red1_s", [128, 1], F32)
        redr = alloc("redr_s", [1, 128], F32)
        rcp1 = alloc("rcp1_s", [128, 4], F32)
        ceps = alloc("ceps_s", [128, 1], F32)
        cesh = alloc("cesh_s", [128, 1], F32)
        def vfill():
            S.wait("vector", vV, S.cnt(vV))

        psA = len(ps_stack)
        ps = palloc("ps_all", [128, 4096], F32)

        def B(b, c0, n):
            return ps[:, 512 * b + c0:512 * b + c0 + n]

        for dst, src in ((ident, identd), (identf, identfd), (onesr, onesd),
                         (mask, maskd), (bq_s, bqf), (bk_s, bkf), (bo_s, bof),
                         (clp, clips)):
            S.op("sync", lambda e, d=dst, s=src: e.dma_start(out=d[:, :], in_=s), dS, 16)

        xn16 = alloc("xn16", [128, DC * TH], BF16)

        sbB1 = len(sb_stack)
        x_tm = alloc("x_tm", [128, 17 * 1024], F32)
        sq_t = alloc("sq_t", [128, 4 * 1024], F32)
        bvrow = alloc("bvrow_s", [1, D], F32)

        S.op("sync", lambda e: e.dma_start(out=bvrow[:, :], in_=bvr), dS, 16)
        dS_xg = {}
        for g in range(4):
            S.op("sync", lambda e, g=g: e.dma_start(
                out=bass.AP(x_tm[:, :].tensor, 4096 * g, [[17 * 1024, 128], [1024, 4], [1, 1024]]),
                in_=bass.AP(xd.tensor, 2 * D + 512 * D * g, [[D, 128], [128 * D, 4], [1, D]])), dS, 16)
            dS_xg[g] = S.cnt(dS)
        S.op("sync", lambda e: e.dma_start(out=x_tm[0:2, 16 * 1024:17 * 1024], in_=xd[0:2, :]), dS, 16)
        S.op("sync", lambda e: e.dma_start(out=x_tm[2:4, 16 * 1024:17 * 1024], in_=xd[2050:2052, :]), dS, 16)
        dS_init = S.cnt(dS)

        S.wait("tensor", dS, dS_init)
        for c in range(2):
            S.op("tensor", lambda e, c=c: e.matmul(
                B(c, 0, 512), onesr[:, :], bvrow[:, 512 * c:512 * c + 512],
                start=True, stop=True), tM, 1)
        S.wait("vector", tM)
        S.op("vector", lambda e: e.tensor_copy(bvb[:, :], ps[:, 0:1024]), vV, 1)
        v_bvb = S.cnt(vV)

        S.op("vector", lambda e: e.memset(ceps[:, :], 1e-5), vV, 1)
        S.op("vector", lambda e: e.memset(cesh[:, :], ESHIFT), vV, 1)
        S.op("vector", lambda e: e.memset(sum_b[:, :], 0.0), vV, 1)
        S.op("vector", lambda e: e.memset(msq_b[:, :], 0.0), vV, 1)
        S.op("vector", lambda e: e.memset(amax_b[:, :], 0.0), vV, 1)
        v_msets = S.cnt(vV)
        S.wait("scalar", vV, v_msets)

        sq_vq = {}
        for t in range(17):
            pn = 128 if t < 16 else 4
            if t < 16 and t % 4 == 0:
                S.wait("vector", dS, dS_xg[t // 4])
                S.wait("scalar", dS, dS_xg[t // 4])
            elif t == 16:
                S.wait("vector", dS, dS_init)
                S.wait("scalar", dS, dS_init)
            if t >= 4:
                S.wait("scalar", vV, sq_vq[t - 4])
            S.op("scalar", lambda e, t=t, pn=pn: e.activation(
                sq_t[0:pn, 1024 * (t % 4):1024 * (t % 4) + 1024],
                x_tm[0:pn, 1024 * t:1024 * t + 1024], AF.Square), sS, 1)
            s_here = S.cnt(sS)
            S.op("vector", lambda e, t=t, pn=pn: e.reduce_sum(
                sum_b[0:pn, t:t + 1], x_tm[0:pn, 1024 * t:1024 * t + 1024], AX.X), vV, 1)
            S.wait("vector", sS, s_here)
            S.op("vector", lambda e, t=t, pn=pn: e.reduce_sum(
                msq_b[0:pn, t:t + 1], sq_t[0:pn, 1024 * (t % 4):1024 * (t % 4) + 1024],
                AX.X), vV, 1)
            sq_vq[t] = S.cnt(vV)
        S.op("vector", lambda e: e.tensor_scalar(
            mean_b[:, :], sum_b[:, :], 1.0 / D, None, ALU.mult), vV, 1)
        vfill()
        S.op("vector", lambda e: e.tensor_scalar(
            msq_b[:, :], msq_b[:, :], 1.0 / D, None, ALU.mult), vV, 1)
        vfill()
        S.op("vector", lambda e: e.tensor_mul(rstd_b[:, :], mean_b[:, :], mean_b[:, :]), vV, 1)
        vfill()
        S.op("vector", lambda e: e.tensor_sub(rstd_b[:, :], msq_b[:, :], rstd_b[:, :]), vV, 1)
        if "stats" in dbgd:
            var_snap = alloc("var_snap", [128, 2 * 17], F32)
            S.op("vector", lambda e: e.tensor_copy(var_snap[:, 0:17], rstd_b[:, :]), vV, 1)
        v_var = S.cnt(vV)
        S.wait("scalar", vV, v_var)
        S.op("scalar", lambda e: e.activation(rstd_b[:, :], rstd_b[:, :], AF.Sqrt,
                                              bias=ceps[:, :]), sS, 1)
        s_sqrt = S.cnt(sS)
        if "stats" in dbgd:
            S.wait("vector", sS, s_sqrt)
            S.op("vector", lambda e: e.tensor_copy(var_snap[:, 17:34], rstd_b[:, :]), vV, 1)
        S.wait("vector", sS, s_sqrt)
        S.op("vector", lambda e: e.reciprocal(rstd_b[:, :], rstd_b[:, :]), vV, 1)
        vfill()

        for t in range(17):
            pn = 128 if t < 16 else 4
            S.op("vector", lambda e, t=t, pn=pn: e.tensor_scalar(
                x_tm[0:pn, 1024 * t:1024 * t + 1024], x_tm[0:pn, 1024 * t:1024 * t + 1024],
                mean_b[0:pn, t:t + 1], rstd_b[0:pn, t:t + 1], ALU.subtract, ALU.mult), vV, 1)
            S.op("vector", lambda e, t=t, pn=pn: e.tensor_reduce(
                amax_b[0:pn, t:t + 1], x_tm[0:pn, 1024 * t:1024 * t + 1024], AX.X,
                ALU.max, apply_absolute_value=True), vV, 1)
        v_ln = S.cnt(vV)

        vfill()
        S.op("vector", lambda e: e.tensor_reduce(red1[:, :], amax_b[:, :], AX.X, ALU.max), vV, 1)
        v_red1 = S.cnt(vV)
        S.wait("tensor", vV, v_red1)
        S.op("tensor", lambda e: e.transpose(B(2, 0, 128)[0:1, :], red1[:, :],
                                             identf[:, :]), tM, 1)
        t_red1 = S.cnt(tM)
        S.wait("vector", tM, t_red1)
        S.op("vector", lambda e: e.tensor_reduce(redr[0:1, 0:1], B(2, 0, 128)[0:1, :],
                                                 AX.X, ALU.max), vV, 1)
        S.wait("gpsimd", vV)
        S.op("gpsimd", lambda e: e.dma_start(out=cc1i.ap(), in_=redr[0:1, 0:1]), dG, 16)
        S.wait("gpsimd", dG)
        S.op("gpsimd", lambda e: e.collective_compute(
            "AllReduce", ALU.max, replica_groups=[list(range(NCORE))],
            ins=[cc1i.ap()], outs=[cc1o.ap()]), cC, 1)
        S.wait("gpsimd", cC)
        S.op("gpsimd", lambda e: e.dma_start(out=sm1[0:1, 0:1], in_=cc1o.ap()), dG, 16)
        dG_g1 = S.cnt(dG)

        if "stats" in dbgd or "xtm" in dbgd:
            S.wait("sync", vV, v_ln)
            S.wait("sync", sS, S.cnt(sS))
            if "stats" in dbgd:
                for i, src in enumerate((sum_b, msq_b, mean_b, rstd_b, amax_b)):
                    S.op("sync", lambda e, i=i, src=src: e.dma_start(
                        out=dbgd["stats"][:, 17 * i:17 * i + 17], in_=src[:, :]), dS, 16)
                S.op("sync", lambda e: e.dma_start(
                    out=dbgd["stats"][:, 85:119], in_=var_snap[:, :]), dS, 16)
            if "xtm" in dbgd:
                S.op("sync", lambda e: e.dma_start(out=dbgd["xtm"], in_=x_tm[:, :]), dS, 16)
            for eng in ("vector", "scalar", "tensor", "gpsimd"):
                S.wait(eng, dS)

        ckpt(11)
        S.wait("tensor", vV)
        v_evA = 0
        s_evB = 0
        for t in range(17):
            pn = 128 if t < 16 else 4
            if t >= 1:
                S.wait("tensor", vV, v_evA)
                S.wait("tensor", sS, s_evB)
            for dc in range(DC):
                S.op("tensor", lambda e, t=t, dc=dc, pn=pn: e.transpose(
                    B(dc, 384, pn),
                    x_tm[0:pn, 1024 * t + 128 * dc:1024 * t + 128 * dc + 128],
                    identf[0:pn, 0:pn]), tM, 1)
            t_tr = S.cnt(tM)
            S.wait("vector", tM, t_tr)
            S.wait("scalar", tM, t_tr)
            if t < 16:
                S.op("vector", lambda e, t=t: e.tensor_copy(
                    bass.AP(xn16[:, :].tensor, 2 + 128 * t, [[DC * TH, 128], [TH, 4], [1, 128]]),
                    bass.AP(ps[:, :].tensor, 384, [[4096, 128], [512, 4], [1, 128]])), vV, 1)
                S.op("scalar", lambda e, t=t: e.activation(
                    bass.AP(xn16[:, :].tensor, 4 * TH + 2 + 128 * t, [[DC * TH, 128], [TH, 4], [1, 128]]),
                    bass.AP(ps[:, :].tensor, 2048 + 384, [[4096, 128], [512, 4], [1, 128]]),
                    AF.Identity), sS, 1)
            else:
                S.op("vector", lambda e: e.tensor_copy(
                    bass.AP(xn16[:, :].tensor, 0, [[DC * TH, 128], [TH, 4], [1, 2]]),
                    bass.AP(ps[:, :].tensor, 384, [[4096, 128], [512, 4], [1, 2]])), vV, 1)
                S.op("vector", lambda e: e.tensor_copy(
                    bass.AP(xn16[:, :].tensor, 2050, [[DC * TH, 128], [TH, 4], [1, 2]]),
                    bass.AP(ps[:, :].tensor, 386, [[4096, 128], [512, 4], [1, 2]])), vV, 1)
                S.op("scalar", lambda e: e.activation(
                    bass.AP(xn16[:, :].tensor, 4 * TH + 0, [[DC * TH, 128], [TH, 4], [1, 2]]),
                    bass.AP(ps[:, :].tensor, 2048 + 384, [[4096, 128], [512, 4], [1, 2]]),
                    AF.Identity), sS, 1)
                S.op("scalar", lambda e: e.activation(
                    bass.AP(xn16[:, :].tensor, 4 * TH + 2050, [[DC * TH, 128], [TH, 4], [1, 2]]),
                    bass.AP(ps[:, :].tensor, 2048 + 386, [[4096, 128], [512, 4], [1, 2]]),
                    AF.Identity), sS, 1)
            v_evA = S.cnt(vV)
            s_evB = S.cnt(sS)
        v_xn16 = S.cnt(vV)
        s_xn16 = S.cnt(sS)

        S.op("scalar", lambda e: e.activation(sm1[0:1, 1:2], clp[0:1, 0:1], AF.Sigmoid), sS, 1)
        s_sig = S.cnt(sS)
        S.wait("vector", dG, dG_g1)
        S.wait("vector", sS, s_sig)
        S.op("vector", lambda e: e.scalar_tensor_tensor(
            sm1[0:1, 2:3], sm1[0:1, 0:1], 1.0 / 127.0, sm1[0:1, 1:2],
            ALU.mult, ALU.mult), vV, 1)
        vfill()
        S.op("vector", lambda e: e.reciprocal(sm1[0:1, 3:4], sm1[0:1, 2:3]), vV, 1)
        v_sx = S.cnt(vV)
        S.wait("tensor", vV, v_sx)
        S.wait("tensor", sS, s_xn16)
        S.op("tensor", lambda e: e.matmul(B(7, 0, 2), onesr[:, :], sm1[0:1, 2:4],
                                          start=True, stop=True), tM, 1)
        S.wait("vector", tM)
        S.op("vector", lambda e: e.tensor_copy(smb[:, 0:2], B(7, 0, 2)), vV, 1)
        vfill()
        S.op("vector", lambda e: e.tensor_scalar(
            c2_b[:, :], rstd_b[:, :], smb[:, 1:2], None, ALU.mult), vV, 1)
        v_c2 = S.cnt(vV)
        if "xn16" in dbgd:
            S.wait("sync", vV, v_xn16)
            S.wait("sync", sS, s_xn16)
            S.op("sync", lambda e: e.dma_start(out=dbgd["xn16"], in_=xn16[:, :]), dS, 16)
            for eng in ("vector", "scalar", "tensor", "gpsimd"):
                S.wait(eng, dS)

        ckpt(21)
        sb_free_to(sbB1)

        sbB2 = len(sb_stack)
        wstg = alloc("wstg", [128, 2 * 4096], BF16)
        q_sb = alloc("q_sb", [128, 4 * T], BF16)
        k_sb = alloc("k_sb", [128, 4 * TH], BF16)
        v_sb = alloc("v_sb", [128, NJ * 520], BF16)
        expm = alloc("expm", [128, 2 * 496], BF16)
        att_j = alloc("att_j", [128, 512], F32)
        rec4 = alloc("rec4", [128, 8], F32)
        attn_fm = alloc("attn_fm", [128, DC * T], BF16)
        s_rot = alloc("s_rot", [128, 2 * T], BF16)
        wnat = alloc("wnat", [128, 4 * 1024], F32)
        wqi = alloc("wqi", [128, 4 * 1024], BF16)

        t_xn_done = S.cnt(tM)
        S.wait("gpsimd", tM, t_xn_done)
        S.wait("sync", tM, t_xn_done)
        S.wait("gpsimd", vV, v_xn16)
        S.wait("gpsimd", sS, s_xn16)
        S.wait("sync", vV, v_xn16)
        S.wait("sync", sS, s_xn16)

        def stage_w(src_ap, half, slot):
            S.op("gpsimd", lambda e, s=src_ap, half=half, slot=slot: e.dma_start(
                out=bass.AP(wstg[:, :].tensor, 4096 * slot, [[2 * 4096, 128], [512, 8], [1, 512]]),
                in_=bass.AP(s.tensor, 512 * half, [[D, 128], [128 * D, 8], [1, 512]])), dG, 16)

        nat_jobs = [("g", ic) for ic in range(NIC)] + [("u", ic) for ic in range(NIC)]
        nat_state = {"i": 0, "vq": {}, "dg": {}, "pend": []}

        def nat_slot():
            i = nat_state["i"]
            if i < len(nat_jobs):
                njobs = min(2, len(nat_jobs) - i)
                nat_state["i"] += njobs
                jobs = []
                for k in range(njobs):
                    kind, ic = nat_jobs[i + k]
                    jobs.append((i + k, kind, ic, (i + k) % 4))
                for idx, kind, ic, sl in jobs:
                    src = wgd if kind == "g" else wud
                    if idx >= 4:
                        S.wait("sync", vV, nat_state["vq"][idx - 4])
                    S.op("sync", lambda e, src=src, ic=ic, sl=sl: e.dma_start(
                        out=wnat[:, 1024 * sl:1024 * sl + 1024],
                        in_=src[128 * ic:128 * ic + 128, :]), dS, 16)
                nat_state["pend"].append((jobs, S.cnt(dS)))
            if not nat_state["pend"]:
                return
            if len(nat_state["pend"]) < 2 and nat_state["i"] < len(nat_jobs):
                return
            jobs, d_here = nat_state["pend"].pop(0)
            njobs = len(jobs)
            S.wait("vector", dS, d_here)
            for idx, kind, ic, sl in jobs:
                if idx >= 4:
                    S.wait("vector", dG, nat_state["dg"][idx - 4])
            for idx, kind, ic, sl in jobs:
                smax = smaxg if kind == "g" else smaxu
                S.op("vector", lambda e, smax=smax, ic=ic, sl=sl: e.tensor_reduce(
                    smax[:, ic:ic + 1], wnat[:, 1024 * sl:1024 * sl + 1024], AX.X,
                    ALU.max, apply_absolute_value=True), vV, 1)
            if njobs == 1:
                vfill()
            for idx, kind, ic, sl in jobs:
                smax = smaxg if kind == "g" else smaxu
                S.op("vector", lambda e, smax=smax, ic=ic, sl=sl: e.reciprocal(
                    rcp1[:, sl:sl + 1], smax[:, ic:ic + 1]), vV, 1)
            vfill()
            for idx, kind, ic, sl in jobs:
                S.op("vector", lambda e, sl=sl: e.tensor_scalar(
                    rcp1[:, sl:sl + 1], rcp1[:, sl:sl + 1], 127.0, None, ALU.mult), vV, 1)
            vfill()
            for idx, kind, ic, sl in jobs:
                S.op("vector", lambda e, sl=sl: e.tensor_scalar(
                    wnat[:, 1024 * sl:1024 * sl + 1024], wnat[:, 1024 * sl:1024 * sl + 1024],
                    rcp1[:, sl:sl + 1], MAGIC, ALU.mult, ALU.add), vV, 1)
            for idx, kind, ic, sl in jobs:
                S.op("vector", lambda e, sl=sl: e.tensor_scalar(
                    wqi[:, 1024 * sl:1024 * sl + 1024], wnat[:, 1024 * sl:1024 * sl + 1024],
                    -MAGIC, None, ALU.add), vV, 1)
            vq_here = S.cnt(vV)
            S.wait("gpsimd", vV, vq_here)
            for idx, kind, ic, sl in jobs:
                scr = scr_g if kind == "g" else scr_u
                nat_state["vq"][idx] = vq_here
                S.op("gpsimd", lambda e, scr=scr, ic=ic, sl=sl: e.dma_start(
                    out=scr.ap()[128 * ic:128 * ic + 128, :],
                    in_=wqi[:, 1024 * sl:1024 * sl + 1024]), dG, 16)
                nat_state["dg"][idx] = S.cnt(dG)

        v_attn_done = 0
        attn_evac = 0
        for half in range(2):
            S.wait("gpsimd", tM)
            stage_w(wqT, half, 0)
            S.wait("tensor", dG)
            S.wait("tensor", vV, v_xn16)
            S.wait("tensor", sS, s_xn16)
            for ec in range(4):
                for dc in range(DC):
                    for t4 in range(4):
                        S.op("tensor", lambda e, ec=ec, dc=dc, t4=t4: e.matmul(
                            B(t4, 0, 512),
                            wstg[:, 512 * dc + 128 * ec:512 * dc + 128 * ec + 128],
                            xn16[:, TH * dc + 2 + 512 * t4:TH * dc + 2 + 512 * t4 + 512],
                            start=(dc == 0), stop=(dc == DC - 1)), tM, 1)
                t_q = S.cnt(tM)
                S.wait("scalar", tM, t_q)
                S.op("scalar", lambda e, ec=ec, half=half: e.activation(
                    q_sb[:, T * ec:T * ec + T], ps[:, 0:2048], AF.Identity,
                    bias=bq_s[:, 4 * half + ec:4 * half + ec + 1]), sS, 1)
                S.wait("tensor", sS)
            t_qdone = S.cnt(tM)
            stage_w(wkT, half, 1)
            S.wait("tensor", dG)
            for ec in range(4):
                for dc in range(DC):
                    for t4 in range(4):
                        S.op("tensor", lambda e, ec=ec, dc=dc, t4=t4: e.matmul(
                            B(t4, 0, 512),
                            wstg[:, 4096 + 512 * dc + 128 * ec:4096 + 512 * dc + 128 * ec + 128],
                            xn16[:, TH * dc + 512 * t4:TH * dc + 512 * t4 + 512],
                            start=(dc == 0), stop=(dc == DC - 1)), tM, 1)
                    S.op("tensor", lambda e, ec=ec, dc=dc: e.matmul(
                        ps[:, 2048:2052],
                        wstg[:, 4096 + 512 * dc + 128 * ec:4096 + 512 * dc + 128 * ec + 128],
                        xn16[:, TH * dc + 2048:TH * dc + 2052],
                        start=(dc == 0), stop=(dc == DC - 1)), tM, 1)
                t_k = S.cnt(tM)
                S.wait("scalar", tM, t_k)
                S.op("scalar", lambda e, ec=ec, half=half: e.activation(
                    k_sb[:, TH * ec:TH * ec + 2048], ps[:, 0:2048], AF.Identity,
                    bias=bk_s[:, 4 * half + ec:4 * half + ec + 1]), sS, 1)
                S.op("scalar", lambda e, ec=ec, half=half: e.activation(
                    k_sb[:, TH * ec + 2048:TH * ec + 2052], ps[:, 2048:2052], AF.Identity,
                    bias=bk_s[:, 4 * half + ec:4 * half + ec + 1]), sS, 1)
                S.wait("tensor", sS)
            S.wait("gpsimd", tM, t_qdone)
            stage_w(wvT, half, 0)
            S.op("vector", lambda e: e.memset(v_sb[:, :], 1.0), vV, 1)
            v_vset = S.cnt(vV)
            S.wait("vector", vV, v_bvb)
            S.wait("tensor", vV, v_vset)
            S.wait("tensor", dG)
            for j in range(NJ):
                zone = j % 4
                kn = min(128, TH - QT * j)
                if j >= 4:
                    S.wait("tensor", vV)
                for dc in range(DC):
                    S.op("tensor", lambda e, j=j, dc=dc, zone=zone, kn=kn: e.matmul(
                        B(zone, 0, 512)[0:kn, :],
                        xn16[:, TH * dc + QT * j:TH * dc + QT * j + kn],
                        wstg[:, 512 * dc:512 * dc + 512],
                        start=(dc == 0), stop=(dc == DC - 1)), tM, 1)
                t_v = S.cnt(tM)
                S.wait("vector", tM, t_v)
                S.op("vector", lambda e, j=j, zone=zone, half=half, kn=kn: e.tensor_tensor(
                    bass.AP(v_sb[:, :].tensor, 520 * j, [[NJ * 520, kn], [65, 8], [1, 64]]),
                    bass.AP(ps[:, :].tensor, 512 * zone, [[4096, kn], [64, 8], [1, 64]]),
                    bass.AP(bvb[:, :].tensor, 512 * half, [[D, kn], [64, 8], [1, 64]]),
                    ALU.add), vV, 1)
            v_vdone = S.cnt(vV)

            if half == 0:
                ckpt(25)
            S.wait("tensor", vV, v_vdone)
            for j in range(NJ):
                qn = QT if j < NJ - 1 else QLAST
                kn = min(128, TH - QT * j)
                p = j % 2
                if attn_evac:
                    S.wait("tensor", vV, attn_evac)
                t_sc0 = 0
                for g in range(2):
                    for hg in range(4):
                        hh = 4 * g + hg
                        ec, pb = hh // 2, 64 * (hh % 2)
                        bank = 4 * g + hg
                        S.op("tensor", lambda e, j=j, ec=ec, pb=pb, bank=bank, p=p, qn=qn, kn=kn: e.matmul(
                            B(bank, 124 * p, qn)[0:kn, :],
                            k_sb[pb:pb + 64, TH * ec + QT * j:TH * ec + QT * j + kn],
                            q_sb[pb:pb + 64, T * ec + QT * j:T * ec + QT * j + qn],
                            start=True, stop=True), tM, 1)
                    if g == 0:
                        t_sc0 = S.cnt(tM)
                t_sc1 = S.cnt(tM)
                S.wait("scalar", tM, t_sc0)
                for hg in range(4):
                    S.op("scalar", lambda e, hg=hg, p=p, qn=qn: e.activation(
                        expm[:, QT * hg:QT * hg + qn], B(hg, 124 * p, qn),
                        AF.Exp, bias=cesh[:, :], scale=0.125), sS, 1)
                s_exp0 = S.cnt(sS)
                S.wait("scalar", tM, t_sc1)
                for hg in range(4):
                    S.op("scalar", lambda e, hg=hg, p=p, qn=qn: e.activation(
                        expm[:, 496 + QT * hg:496 + QT * hg + qn], B(4 + hg, 124 * p, qn),
                        AF.Exp, bias=cesh[:, :], scale=0.125), sS, 1)
                s_exp1 = S.cnt(sS)
                S.wait("vector", sS, s_exp0)
                S.op("vector", lambda e, j=j, qn=qn: e.tensor_tensor(
                    bass.AP(expm[:, :].tensor, 0, [[2 * 496, 128], [QT, 4], [1, qn]]),
                    bass.AP(expm[:, :].tensor, 0, [[2 * 496, 128], [QT, 4], [1, qn]]),
                    bass.AP(mask[:, :].tensor, QT * j, [[NJ * QT, 128], [0, 4], [1, qn]]),
                    ALU.mult), vV, 1)
                v_m0 = S.cnt(vV)
                S.wait("vector", sS, s_exp1)
                S.op("vector", lambda e, j=j, qn=qn: e.tensor_tensor(
                    bass.AP(expm[:, :].tensor, 496, [[2 * 496, 128], [QT, 4], [1, qn]]),
                    bass.AP(expm[:, :].tensor, 496, [[2 * 496, 128], [QT, 4], [1, qn]]),
                    bass.AP(mask[:, :].tensor, QT * j, [[NJ * QT, 128], [0, 4], [1, qn]]),
                    ALU.mult), vV, 1)
                v_m1 = S.cnt(vV)
                S.wait("tensor", vV, v_m0)
                for hg in range(4):
                    S.op("tensor", lambda e, j=j, hg=hg, p=p, qn=qn: e.matmul(
                        B(hg, 248 + 68 * p, 65)[0:qn, :],
                        expm[:, QT * hg:QT * hg + qn],
                        v_sb[:, 520 * j + 65 * hg:520 * j + 65 * hg + 65],
                        start=True, stop=True), tM, 1)
                t_av0 = S.cnt(tM)
                S.wait("tensor", vV, v_m1)
                for hg in range(4):
                    S.op("tensor", lambda e, j=j, hg=hg, p=p, qn=qn: e.matmul(
                        B(4 + hg, 248 + 68 * p, 65)[0:qn, :],
                        expm[:, 496 + QT * hg:496 + QT * hg + qn],
                        v_sb[:, 520 * j + 65 * (4 + hg):520 * j + 65 * (4 + hg) + 65],
                        start=True, stop=True), tM, 1)
                t_av1 = S.cnt(tM)
                for g, t_av in ((0, t_av0), (1, t_av1)):
                    S.wait("vector", tM, t_av)
                    off = 2048 * g + 248 + 68 * p + 64
                    S.op("vector", lambda e, g=g, off=off, qn=qn: e.tensor_scalar(
                        rec4[0:qn, 4 * g:4 * g + 4],
                        bass.AP(ps[:, :].tensor, off, [[4096, qn], [512, 4]]),
                        SUMCLAMP, None, ALU.max), vV, 1)
                vfill()
                for g in range(2):
                    S.op("vector", lambda e, g=g, qn=qn: e.reciprocal(
                        rec4[0:qn, 4 * g:4 * g + 4], rec4[0:qn, 4 * g:4 * g + 4]), vV, 1)
                vfill()
                for g in range(2):
                    for hg in range(4):
                        hh = 4 * g + hg
                        S.op("vector", lambda e, g=g, hg=hg, hh=hh, p=p, qn=qn: e.tensor_scalar(
                            att_j[0:qn, 64 * hh:64 * hh + 64],
                            B(4 * g + hg, 248 + 68 * p, 64)[0:qn, :],
                            rec4[0:qn, 4 * g + hg:4 * g + hg + 1], None, ALU.mult), vV, 1)
                v_attj = S.cnt(vV)
                S.wait("tensor", vV, v_attj)
                for cg in range(4):
                    S.op("tensor", lambda e, cg=cg, qn=qn: e.transpose(
                        B(cg, 384, qn),
                        att_j[0:qn, 128 * cg:128 * cg + 128], identf[0:qn, 0:qn]), tM, 1)
                t_tp = S.cnt(tM)
                S.wait("vector", tM, t_tp)
                S.op("vector", lambda e, j=j, qn=qn, half=half: e.tensor_copy(
                    bass.AP(attn_fm[:, :].tensor, T * 4 * half + QT * j,
                            [[DC * T, 128], [T, 4], [1, qn]]),
                    bass.AP(ps[:, :].tensor, 384, [[4096, 128], [512, 4], [1, qn]])), vV, 1)
                attn_evac = S.cnt(vV)
                nat_slot()
                nat_slot()
            v_attn_done = S.cnt(vV)
            if half == 0:
                ckpt(26)

            dumped = False
            for k_, buf in (("q", q_sb), ("k", k_sb), ("v", v_sb)):
                if k_ in dbgd and half == 0:
                    S.wait("sync", vV, v_attn_done)
                    S.op("sync", lambda e, b=buf, k_=k_: e.dma_start(out=dbgd[k_], in_=b[:, :]), dS, 16)
                    dumped = True
            if dumped:
                for eng in ("vector", "scalar", "tensor", "gpsimd"):
                    S.wait(eng, dS)

        if "attnfm" in dbgd:
            S.wait("sync", vV, v_attn_done)
            S.op("sync", lambda e: e.dma_start(out=dbgd["attnfm"], in_=attn_fm[:, :]), dS, 16)
            for eng in ("vector", "scalar", "tensor", "gpsimd"):
                S.wait(eng, dS)

        ckpt(31)
        dS_spill = {}
        for half in range(2):
            S.wait("gpsimd", tM)
            stage_w(woT, half, 1)
            S.wait("tensor", dG)
            S.wait("tensor", vV, v_attn_done)
            for ec in range(4):
                ecg = 4 * half + ec
                for dc in range(DC):
                    for t4 in range(4):
                        S.op("tensor", lambda e, ec=ec, dc=dc, t4=t4: e.matmul(
                            B(t4, 0, 512),
                            wstg[:, 4096 + 512 * dc + 128 * ec:4096 + 512 * dc + 128 * ec + 128],
                            attn_fm[:, T * dc + 512 * t4:T * dc + 512 * t4 + 512],
                            start=(dc == 0), stop=(dc == DC - 1)), tM, 1)
                t_wo = S.cnt(tM)
                S.wait("scalar", tM, t_wo)
                if ecg >= 2:
                    S.wait("scalar", dS, dS_spill.get(ecg - 2, 0))
                S.op("scalar", lambda e, ecg=ecg: e.activation(
                    s_rot[:, T * (ecg % 2):T * (ecg % 2) + T], ps[:, 0:2048],
                    AF.Identity, bias=bo_s[:, ecg:ecg + 1]), sS, 1)
                s_here = S.cnt(sS)
                S.wait("sync", sS, s_here)
                S.op("sync", lambda e, ecg=ecg: e.dma_start(
                    out=s_dram.ap()[:, T * ecg:T * ecg + T],
                    in_=s_rot[:, T * (ecg % 2):T * (ecg % 2) + T]), dS, 16)
                dS_spill[ecg] = S.cnt(dS)
                S.wait("tensor", sS)
        t_wo_all = S.cnt(tM)
        s_wo_all = S.cnt(sS)
        dS_s_spilled = S.cnt(dS)

        ckpt(41)
        sb_free_to(sbB2 - 1)

        sbB3 = len(sb_stack)
        xq_fm = alloc("xq_fm", [128, DC * T], BF16)
        xrl = alloc("xrl", [128, 2 * 1024], F32)
        xq_t = alloc("xq_t", [128, 1024], F32)
        wgT_s = alloc("wgT_s", [128, DC * IFF], BF16)
        wuT_s = alloc("wuT_s", [128, DC * IFF], BF16)
        sig_b = alloc("sig_b", [128, 2048], F32)
        g_sb = alloc("g_sb", [128, 2048], F32)
        uf = alloc("uf", [128, 4 * 1024], F32)
        wdnat = alloc("wdnat", [128, IFF], F32)
        wdi = alloc("wdi", [128, IFF], BF16)

        S.wait("sync", tM, t_wo_all)
        S.wait("sync", dS, dS_s_spilled)
        S.wait("sync", sS, s_wo_all)
        S.wait("gpsimd", tM, t_wo_all)
        S.wait("gpsimd", sS, s_wo_all)

        while nat_state["pend"]:
            nat_slot()
        dG_nat_all = S.cnt(dG)
        S.wait("sync", dG, dG_nat_all)
        wT_hc = {}
        for hc in range(DC):
            S.op("sync", lambda e, hc=hc: e.dma_start(
                out=wgT_s[:, IFF * hc:IFF * hc + IFF],
                in_=scr_g.ap()[:, 128 * hc:128 * hc + 128], transpose=True), dS, 16)
            S.op("sync", lambda e, hc=hc: e.dma_start(
                out=wuT_s[:, IFF * hc:IFF * hc + IFF],
                in_=scr_u.ap()[:, 128 * hc:128 * hc + 128], transpose=True), dS, 16)
            wT_hc[hc] = S.cnt(dS)
        dS_wT = S.cnt(dS)

        S.wait("vector", vV, v_c2)
        S.wait("tensor", sS, s_wo_all)
        t_xq_tp = S.cnt(tM)
        v_xq_ev = 0
        for t in range(16):
            if t >= 2:
                S.wait("gpsimd", vV)
            S.op("gpsimd", lambda e, t=t: e.dma_start(
                out=xrl[:, 1024 * (t % 2):1024 * (t % 2) + 1024],
                in_=xd[2 + 128 * t:2 + 128 * t + 128, :]), dG, 16)
            d_x = S.cnt(dG)
            S.wait("vector", dG, d_x)
            if t >= 1:
                S.wait("vector", tM, t_xq_tp)
            S.op("vector", lambda e, t=t: e.tensor_scalar(
                xrl[:, 1024 * (t % 2):1024 * (t % 2) + 1024],
                xrl[:, 1024 * (t % 2):1024 * (t % 2) + 1024],
                mean_b[:, t:t + 1], c2_b[:, t:t + 1], ALU.subtract, ALU.mult), vV, 1)
            S.op("vector", lambda e, t=t: e.tensor_scalar(
                xrl[:, 1024 * (t % 2):1024 * (t % 2) + 1024],
                xrl[:, 1024 * (t % 2):1024 * (t % 2) + 1024],
                MAGIC, -MAGIC, ALU.add, ALU.add), vV, 1)
            S.op("vector", lambda e, t=t: e.tensor_scalar(
                xq_t[:, :], xrl[:, 1024 * (t % 2):1024 * (t % 2) + 1024],
                -128.0, 127.0, ALU.max, ALU.min), vV, 1)
            v_rq = S.cnt(vV)
            S.wait("tensor", vV, v_rq)
            if v_xq_ev:
                S.wait("tensor", vV, v_xq_ev)
            for dc in range(DC):
                S.op("tensor", lambda e, dc=dc: e.transpose(
                    B(dc, 384, 128), xq_t[:, 128 * dc:128 * dc + 128],
                    identf[:, :]), tM, 1)
            t_xq_tp = S.cnt(tM)
            S.wait("vector", tM, t_xq_tp)
            S.op("vector", lambda e, t=t: e.tensor_copy(
                bass.AP(xq_fm[:, :].tensor, 128 * t, [[DC * T, 128], [T, 8], [1, 128]]),
                bass.AP(ps[:, :].tensor, 384, [[4096, 128], [512, 8], [1, 128]])), vV, 1)
            v_xq_ev = S.cnt(vV)
        v_xq_done = S.cnt(vV)
        if "xq" in dbgd:
            S.wait("sync", vV, v_xq_done)
            S.op("sync", lambda e: e.dma_start(out=dbgd["xq"], in_=xq_fm[:, :]), dS, 16)
            for eng in ("vector", "scalar", "tensor", "gpsimd"):
                S.wait(eng, dS)

        ckpt(51)
        for hc in range(DC):
            if hc >= 1:
                S.wait("gpsimd", vV)
            S.op("gpsimd", lambda e, hc=hc: e.dma_start(
                out=wdnat[:, :], in_=wdd[128 * hc:128 * hc + 128, :]), dG, 16)
            d_wd = S.cnt(dG)
            S.wait("vector", dG, d_wd)
            S.op("vector", lambda e, hc=hc: e.tensor_reduce(
                smaxd[:, hc:hc + 1], wdnat[:, :], AX.X, ALU.max,
                apply_absolute_value=True), vV, 1)
            vfill()
            S.op("vector", lambda e, hc=hc: e.reciprocal(rcp1[:, 0:1], smaxd[:, hc:hc + 1]), vV, 1)
            vfill()
            S.op("vector", lambda e: e.tensor_scalar(
                rcp1[:, 0:1], rcp1[:, 0:1], 127.0, None, ALU.mult), vV, 1)
            vfill()
            S.op("vector", lambda e: e.tensor_scalar(
                wdnat[:, :], wdnat[:, :], rcp1[:, 0:1], MAGIC, ALU.mult, ALU.add), vV, 1)
            S.op("vector", lambda e: e.tensor_scalar(
                wdi[:, :], wdnat[:, :], -MAGIC, None, ALU.add), vV, 1)
            v_wd = S.cnt(vV)
            S.wait("gpsimd", vV, v_wd)
            S.op("gpsimd", lambda e, hc=hc: e.dma_start(
                out=scr_d.ap()[128 * hc:128 * hc + 128, :], in_=wdi[:, :]), dG, 16)
        dG_wd_scr = S.cnt(dG)

        S.op("vector", lambda e: e.tensor_scalar(
            ag_b[:, :], smaxg[:, :], smb[:, 0:1], 1.0 / 127.0, ALU.mult, ALU.mult), vV, 1)
        S.op("vector", lambda e: e.tensor_scalar(
            au_b[:, :], smaxu[:, :], smb[:, 0:1], 1.0 / 127.0, ALU.mult, ALU.mult), vV, 1)
        vfill()
        S.op("vector", lambda e: e.tensor_mul(agau[:, :], ag_b[:, :], au_b[:, :]), vV, 1)
        v_agau = S.cnt(vV)

        ckpt(61)
        S.wait("tensor", vV, v_xq_done)
        ps_free_to(psA)
        psB = len(ps_stack)
        ps_gp = (palloc("ps_g0", [128, 1024], F32), palloc("ps_g1", [128, 1024], F32))
        ps_up = (palloc("ps_u0", [128, 1024], F32), palloc("ps_u1", [128, 1024], F32))

        v_evac = {}
        s_sigm = {}
        dG_usp = {}
        pend_merge = None
        gustop = 2 * NIC if (stop_phase is None or stop_phase < 300) else (stop_phase - 300)
        for step in range(min(2 * NIC, gustop)):
            ic, th = step // 2, step % 2
            pp = step % 2
            pg, pu = ps_gp[pp], ps_up[pp]
            if step >= 2:
                S.wait("tensor", vV, v_evac[step - 2])
                S.wait("tensor", sS, s_sigm[step - 2])
            for hc in range(DC):
                if step == 0:
                    S.wait("tensor", dS, wT_hc[hc])
                for t4 in range(2):
                    S.op("tensor", lambda e, ic=ic, th=th, hc=hc, t4=t4, pg=pg: e.matmul(
                        pg[:, 512 * t4:512 * t4 + 512],
                        wgT_s[:, IFF * hc + 128 * ic:IFF * hc + 128 * ic + 128],
                        xq_fm[:, T * hc + 1024 * th + 512 * t4:T * hc + 1024 * th + 512 * t4 + 512],
                        start=(hc == 0), stop=(hc == DC - 1)), tM, 1)
            for hc in range(DC):
                for t4 in range(2):
                    S.op("tensor", lambda e, ic=ic, th=th, hc=hc, t4=t4, pu=pu: e.matmul(
                        pu[:, 512 * t4:512 * t4 + 512],
                        wuT_s[:, IFF * hc + 128 * ic:IFF * hc + 128 * ic + 128],
                        xq_fm[:, T * hc + 1024 * th + 512 * t4:T * hc + 1024 * th + 512 * t4 + 512],
                        start=(hc == 0), stop=(hc == DC - 1)), tM, 1)
            t_gu = S.cnt(tM)
            S.wait("scalar", tM, t_gu)
            if step == 0:
                S.wait("scalar", vV, v_agau)
            S.op("scalar", lambda e, pp=pp, pg=pg: e.activation(
                g_sb[:, 1024 * pp:1024 * pp + 1024], pg[:, :], AF.Identity), sS, 1)
            s_gev = S.cnt(sS)
            S.op("scalar", lambda e, ic=ic, pp=pp: e.activation(
                sig_b[:, 1024 * pp:1024 * pp + 1024],
                g_sb[:, 1024 * pp:1024 * pp + 1024], AF.Sigmoid,
                scale=ag_b[:, ic:ic + 1]), sS, 1)
            s_sigm[step] = S.cnt(sS)
            S.wait("vector", tM, t_gu)
            if step >= 4:
                S.wait("vector", dG, dG_usp[step - 4])
            uo = 1024 * (step % 4)
            S.op("vector", lambda e, uo=uo, pu=pu: e.tensor_copy(
                uf[:, uo:uo + 1024], pu[:, :]), vV, 1)
            if pend_merge is not None:
                S.op("vector", lambda e, ic_=pend_merge: e.tensor_tensor(
                    wmax[:, ic_:ic_ + 1], wmax[:, ic_:ic_ + 1], rcp1[:, 0:1],
                    ALU.max), vV, 1)
                pend_merge = None
            S.wait("vector", sS, s_gev)
            S.op("vector", lambda e, uo=uo, pp=pp: e.tensor_mul(
                uf[:, uo:uo + 1024], g_sb[:, 1024 * pp:1024 * pp + 1024],
                uf[:, uo:uo + 1024]), vV, 1)
            S.wait("vector", sS, s_sigm[step])
            S.op("vector", lambda e, uo=uo, pp=pp: e.tensor_mul(
                uf[:, uo:uo + 1024], uf[:, uo:uo + 1024],
                sig_b[:, 1024 * pp:1024 * pp + 1024]), vV, 1)
            S.op("vector", lambda e, ic=ic, th=th, uo=uo: e.tensor_reduce(
                wmax[:, ic:ic + 1] if th == 0 else rcp1[:, 0:1],
                uf[:, uo:uo + 1024], AX.X, ALU.max, apply_absolute_value=True), vV, 1)
            if th == 1:
                pend_merge = ic
            v_evac[step] = S.cnt(vV)
            S.wait("gpsimd", vV, v_evac[step])
            S.op("gpsimd", lambda e, ic=ic, th=th, uo=uo: e.dma_start(
                out=u_dram.ap()[128 * ic:128 * ic + 128, 1024 * th:1024 * th + 1024],
                in_=uf[:, uo:uo + 1024]), dG, 16)
            dG_usp[step] = S.cnt(dG)
        t_gu_all = S.cnt(tM)
        if pend_merge is not None:
            vfill()
            S.op("vector", lambda e, ic_=pend_merge: e.tensor_tensor(
                wmax[:, ic_:ic_ + 1], wmax[:, ic_:ic_ + 1], rcp1[:, 0:1],
                ALU.max), vV, 1)
            pend_merge = None
            vfill()
        if stop_phase is not None and 300 <= stop_phase < 400:
            ckpt(299)
        if "wmax" in dbgd:
            S.wait("sync", vV)
            S.op("sync", lambda e: e.dma_start(out=dbgd["wmax"], in_=wmax[:, :]), dS, 16)
            for eng in ("vector", "scalar", "tensor", "gpsimd"):
                S.wait(eng, dS)

        ckpt(71)
        S.op("vector", lambda e: e.tensor_mul(ccb[:, :], wmax[:, :], agau[:, :]), vV, 1)
        vfill()
        S.op("vector", lambda e: e.tensor_reduce(red1[:, :], ccb[:, :], AX.X, ALU.max), vV, 1)
        v_um = S.cnt(vV)
        S.wait("tensor", vV, v_um)
        S.op("tensor", lambda e: e.transpose(ps_up[1][0:1, 0:128], red1[:, :],
                                             identf[:, :]), tM, 1)
        t_um = S.cnt(tM)
        S.wait("vector", tM, t_um)
        S.op("vector", lambda e: e.tensor_reduce(redr[0:1, 0:1], ps_up[1][0:1, 0:128],
                                                 AX.X, ALU.max), vV, 1)
        v_um2 = S.cnt(vV)
        S.wait("gpsimd", vV, v_um2)
        S.op("gpsimd", lambda e: e.dma_start(out=cc2i.ap(), in_=redr[0:1, 0:1]), dG, 16)
        S.wait("gpsimd", dG)
        S.op("gpsimd", lambda e: e.collective_compute(
            "AllReduce", ALU.max, replica_groups=[list(range(NCORE))],
            ins=[cc2i.ap()], outs=[cc2o.ap()]), cC, 1)
        S.wait("gpsimd", cC)
        S.op("gpsimd", lambda e: e.dma_start(out=sm1[0:1, 4:5], in_=cc2o.ap()), dG, 16)
        dG_g2 = S.cnt(dG)

        S.op("scalar", lambda e: e.activation(sm1[0:1, 5:6], clp[0:1, 1:2], AF.Sigmoid), sS, 1)
        s_sig2 = S.cnt(sS)
        S.wait("vector", dG, dG_g2)
        S.wait("vector", sS, s_sig2)
        S.op("vector", lambda e: e.scalar_tensor_tensor(
            sm1[0:1, 6:7], sm1[0:1, 4:5], 1.0 / 127.0, sm1[0:1, 5:6],
            ALU.mult, ALU.mult), vV, 1)
        vfill()
        S.op("vector", lambda e: e.reciprocal(sm1[0:1, 7:8], sm1[0:1, 6:7]), vV, 1)
        v_su = S.cnt(vV)
        S.wait("tensor", vV, v_su)
        S.wait("tensor", vV, v_evac[2 * NIC - 1])
        S.op("tensor", lambda e: e.matmul(ps_up[0][:, 0:2], onesr[:, :], sm1[0:1, 6:8],
                                          start=True, stop=True), tM, 1)
        t_smb2 = S.cnt(tM)
        S.wait("vector", tM, t_smb2)
        S.op("vector", lambda e: e.tensor_copy(smb[:, 2:4], ps_up[0][:, 0:2]), vV, 1)
        vfill()
        S.op("vector", lambda e: e.tensor_scalar(
            ccb[:, :], agau[:, :], smb[:, 3:4], None, ALU.mult), vV, 1)
        S.op("vector", lambda e: e.tensor_scalar(
            sdsu[:, :], smaxd[:, :], smb[:, 2:3], 1.0 / 127.0, ALU.mult, ALU.mult), vV, 1)
        v_ffs = S.cnt(vV)

        ckpt(81)
        sb_free_to(sbB3)

        sbB4 = len(sb_stack)
        h_bf = alloc("h_bf", [128, NIC * T], BF16)
        wdT_s = alloc("wdT_s", [128, NIC * 1024], BF16)
        url = alloc("url", [128, 2 * 2048], F32)
        s_rl = alloc("s_rl", [128, 4 * T], BF16)
        sg_rot = alloc("sg_rot", [128, 2 * T], F32)
        sig2 = alloc("sig2", [128, 2 * 2048], F32)

        S.wait("sync", tM, t_gu_all)
        S.wait("sync", vV, v_ffs)
        S.wait("gpsimd", tM, t_gu_all)
        S.wait("gpsimd", vV, v_ffs)

        S.wait("sync", dG, dG_wd_scr)
        for ic in range(NIC):
            S.op("sync", lambda e, ic=ic: e.dma_start(
                out=wdT_s[:, 1024 * ic:1024 * ic + 1024],
                in_=scr_d.ap()[:, 128 * ic:128 * ic + 128], transpose=True), dS, 16)
        dS_wdT = S.cnt(dS)

        v_rnd = {}
        for ic in range(NIC):
            sl = ic % 2
            eng, sem = ("gpsimd", dG) if sl == 0 else ("sync", dS)
            if sl == 1:
                S.wait("sync", dG, dG_usp[2 * ic + 1])
            if ic >= 2:
                S.wait(eng, vV, v_rnd[ic - 2])
            S.op(eng, lambda e, ic=ic, sl=sl: e.dma_start(
                out=url[:, 2048 * sl:2048 * sl + 2048],
                in_=u_dram.ap()[128 * ic:128 * ic + 128, :]), sem, 16)
            d_u = S.cnt(sem)
            S.wait("vector", sem, d_u)
            S.wait("vector", vV, v_ffs)
            S.op("vector", lambda e, ic=ic, sl=sl: e.tensor_scalar(
                url[:, 2048 * sl:2048 * sl + 2048], url[:, 2048 * sl:2048 * sl + 2048],
                ccb[:, ic:ic + 1], MAGIC, ALU.mult, ALU.add), vV, 1)
            S.op("vector", lambda e, sl=sl: e.tensor_scalar(
                url[:, 2048 * sl:2048 * sl + 2048], url[:, 2048 * sl:2048 * sl + 2048],
                -MAGIC, -128.0, ALU.add, ALU.max), vV, 1)
            S.op("vector", lambda e, ic=ic, sl=sl: e.tensor_scalar(
                h_bf[:, T * ic:T * ic + T], url[:, 2048 * sl:2048 * sl + 2048],
                127.0, None, ALU.min), vV, 1)
            v_rnd[ic] = S.cnt(vV)
        v_rnd_all = S.cnt(vV)
        if "h" in dbgd:
            S.wait("sync", vV, v_rnd_all)
            S.op("sync", lambda e: e.dma_start(out=dbgd["h"], in_=h_bf[:, :]), dS, 16)
            for eng in ("vector", "scalar", "tensor", "gpsimd"):
                S.wait(eng, dS)

        if "scrd" in dbgd:
            S.wait("sync", dG, dG_wd_scr)
            S.op("sync", lambda e: e.dma_start(out=dbgd["scrd"], in_=scr_d.ap()), dS, 16)
            for eng in ("vector", "scalar", "tensor", "gpsimd"):
                S.wait(eng, dS)
        if "wdt" in dbgd:
            S.wait("sync", dS, dS_wdT)
            S.op("sync", lambda e: e.dma_start(out=dbgd["wdt"], in_=wdT_s[:, :]), dS, 16)
            for eng in ("vector", "scalar", "tensor", "gpsimd"):
                S.wait(eng, dS)

        ckpt(91)
        ps_free_to(psB)
        psC = len(ps_stack)
        ps_d0 = palloc("ps_d0", [128, 2048], F32)
        ps_d1 = palloc("ps_d1", [128, 2048], F32)
        ps_dx = (ps_d0, ps_d1)

        S.wait("tensor", dS, dS_wdT)
        S.wait("tensor", vV, v_rnd_all)
        v_sg = {}
        s_sg = {}
        dS_sgsp = {}
        d_s = {}

        def srl_load(hc):
            sl4 = hc % 4
            if hc >= 4:
                S.wait("sync", vV, v_sg[hc - 4])
            S.op("sync", lambda e, hc=hc, sl4=sl4: e.dma_start(
                out=s_rl[:, T * sl4:T * sl4 + T],
                in_=s_dram.ap()[:, T * hc:T * hc + T]), dS, 16)
            d_s[hc] = S.cnt(dS)

        srl_load(0)
        srl_load(1)
        for hc in range(DC):
            pp = ps_dx[hc % 2]
            if hc >= 2:
                S.wait("tensor", vV, v_sg[hc - 2])
                S.wait("tensor", sS, s_sg[hc - 2])
            for ic in range(NIC):
                for t4 in range(4):
                    S.op("tensor", lambda e, hc=hc, ic=ic, t4=t4, pp=pp: e.matmul(
                        pp[:, 512 * t4:512 * t4 + 512],
                        wdT_s[:, 1024 * ic + 128 * hc:1024 * ic + 128 * hc + 128],
                        h_bf[:, T * ic + 512 * t4:T * ic + 512 * t4 + 512],
                        start=(ic == 0), stop=(ic == NIC - 1)), tM, 1)
            t_d = S.cnt(tM)
            sl = hc % 2
            if hc + 2 < DC:
                srl_load(hc + 2)
            S.wait("scalar", tM, t_d)
            S.op("scalar", lambda e, hc=hc, sl=sl, pp=pp: e.activation(
                sig2[:, 2048 * sl:2048 * sl + 2048], pp[:, :], AF.Sigmoid,
                scale=sdsu[:, hc:hc + 1]), sS, 1)
            s_sg[hc] = S.cnt(sS)
            S.wait("vector", sS, s_sg[hc])
            S.wait("vector", dS, d_s[hc])
            if hc >= 2:
                S.wait("vector", dS, dS_sgsp.get(hc - 2, 0))
            S.op("vector", lambda e, hc=hc, sl=sl, pp=pp: e.scalar_tensor_tensor(
                sig2[:, 2048 * sl:2048 * sl + 2048], pp[:, :], sdsu[:, hc:hc + 1],
                sig2[:, 2048 * sl:2048 * sl + 2048], ALU.mult, ALU.mult), vV, 1)
            S.op("vector", lambda e, hc=hc, sl=sl: e.tensor_mul(
                sg_rot[:, T * sl:T * sl + T], sig2[:, 2048 * sl:2048 * sl + 2048],
                s_rl[:, T * (hc % 4):T * (hc % 4) + T]), vV, 1)
            v_sg[hc] = S.cnt(vV)
            S.wait("sync", vV, v_sg[hc])
            S.op("sync", lambda e, hc=hc, sl=sl: e.dma_start(
                out=sg_dram.ap()[:, T * hc:T * hc + T],
                in_=sg_rot[:, T * sl:T * sl + T]), dS, 16)
            dS_sgsp[hc] = S.cnt(dS)
        v_sg_all = S.cnt(vV)
        dS_sg_all = S.cnt(dS)
        if "sg" in dbgd:
            S.wait("sync", dS, dS_sg_all)
            S.op("sync", lambda e: e.dma_start(out=dbgd["sg"], in_=sg_dram.ap()), dS, 16)
            for eng in ("vector", "scalar", "tensor", "gpsimd"):
                S.wait(eng, dS)
        if "s" in dbgd:
            S.wait("sync", dS, dS_sg_all)
            S.op("sync", lambda e: e.dma_start(out=dbgd["s"], in_=s_dram.ap()), dS, 16)
            for eng in ("vector", "scalar", "tensor", "gpsimd"):
                S.wait(eng, dS)
        if "sm1" in dbgd:
            S.wait("sync", vV, v_sg_all)
            S.op("sync", lambda e: e.dma_start(out=dbgd["sm1"], in_=sm1[:, :]), dS, 16)
            for eng in ("vector", "scalar", "tensor", "gpsimd"):
                S.wait(eng, dS)

        for _pi in range(pad_sync):
            if pad_kind == "dma":
                S.op("sync", lambda e: e.dma_start(out=u_dram.ap()[0:1, 0:16],
                                                   in_=u_dram.ap()[2:3, 0:16]), dS, 16)
            else:
                S.wait("sync", vV, 1)
        if pad_sync:
            S.wait("sync", dS)
        ckpt(95)
        sb_free_to(sbB4)
        ps_free_to(psC)
        ps_f = palloc("ps_f", [128, 4096], F32)
        sgt = alloc("sgt", [128, 4 * 1024], F32)
        xfin = alloc("xfin", [128, 4 * 1024], F32)
        outb = alloc("outb", [128, 4 * 1024], F32)

        S.wait("tensor", vV, v_sg_all)
        S.wait("sync", dS, dS_sg_all)
        S.wait("gpsimd", dS, dS_sg_all)
        dS_out = {}
        dG_sgt = {}
        t_f_hist = {}
        v_add = 0
        tstop = 16 if (stop_phase is None or stop_phase < 101) else (stop_phase - 101)
        for t in range(min(16, tstop)):
            sl = t % 4
            if t >= 4:
                S.wait("sync", dS, dS_out[t - 4])
                S.wait("gpsimd", tM, t_f_hist[t - 4])
            if "sgt" not in dskip:
                for cg in range(DC):
                    S.op("gpsimd", lambda e, t=t, sl=sl, cg=cg: e.dma_start(
                        out=sgt[:, 1024 * sl + 128 * cg:1024 * sl + 128 * cg + 128],
                        in_=sg_dram.ap()[:, T * cg + 128 * t:T * cg + 128 * t + 128]), dG, 16)
            dG_sgt[t] = S.cnt(dG)
            if "xfin" in dskip:
                S.op("sync", lambda e, t=t, sl=sl: e.dma_start(
                    out=xfin[:, 1024 * sl:1024 * sl + 1024],
                    in_=u_dram.ap()[0:128, 0:1024]), dS, 16)
            else:
                S.op("sync", lambda e, t=t, sl=sl: e.dma_start(
                    out=xfin[:, 1024 * sl:1024 * sl + 1024],
                    in_=xd[2 + 128 * t:2 + 128 * t + 128, :]), dS, 16)
            d_xf = S.cnt(dS)
            if v_add:
                S.wait("tensor", vV, v_add)
            S.wait("tensor", dG, dG_sgt[t])
            if "tp" not in dskip:
                for cg in range(DC):
                    S.op("tensor", lambda e, sl=sl, cg=cg: e.transpose(
                        ps_f[:, 512 * cg + 384:512 * cg + 384 + 128],
                        sgt[:, 1024 * sl + 128 * cg:1024 * sl + 128 * cg + 128],
                        identf[:, :]), tM, 1)
            t_f = S.cnt(tM)
            t_f_hist[t] = t_f
            S.wait("vector", tM, t_f)
            S.wait("vector", dS, d_xf)
            for hb in range(2):
                if "add" in dskip:
                    S.op("vector", lambda e, sl=sl, hb=hb: e.memset(
                        outb[:, 1024 * sl + 512 * hb:1024 * sl + 512 * hb + 512], 0.5), vV, 1)
                else:
                    S.op("vector", lambda e, sl=sl, hb=hb: e.tensor_add(
                        bass.AP(outb[:, :].tensor, 1024 * sl + 512 * hb, [[4096, 128], [128, 4], [1, 128]]),
                        bass.AP(ps_f[:, :].tensor, 2048 * hb + 384, [[4096, 128], [512, 4], [1, 128]]),
                        bass.AP(xfin[:, :].tensor, 1024 * sl + 512 * hb, [[4096, 128], [128, 4], [1, 128]])), vV, 1)
            v_add = S.cnt(vV)
            S.wait("sync", vV, v_add)
            if "out" in dskip:
                S.op("sync", lambda e, t=t, sl=sl: e.dma_start(
                    out=u_dram.ap()[128:256, 0:1024],
                    in_=outb[:, 1024 * sl:1024 * sl + 1024]), dS, 16)
            else:
                S.op("sync", lambda e, t=t, sl=sl: e.dma_start(
                    out=outd[128 * t:128 * t + 128, :],
                    in_=outb[:, 1024 * sl:1024 * sl + 1024]), dS, 16)
            dS_out[t] = S.cnt(dS)
        if stop_phase is not None and stop_phase >= 101:
            ckpt(200)
        S.wait("sync", dS)

    except _StopBuild:
        pass
    sb_free_to(0)
    ps_free_to(0)

    with nc.Block() as block:

        @block.sync
        def _(sync):
            S.emit("sync", sync)

        @block.tensor
        def _(tensor):
            S.emit("tensor", tensor)

        @block.vector
        def _(vector):
            S.emit("vector", vector)

        @block.scalar
        def _(scalar):
            S.emit("scalar", scalar)

        @block.gpsimd
        def _(gpsimd):
            S.emit("gpsimd", gpsimd)

    for cm in reversed(sems_cm):
        cm.__exit__(None, None, None)
    return nc


_NC_CACHE = {}


def _get_nc(dbg=()):
    key = tuple(sorted(dbg))
    if key not in _NC_CACHE:
        _NC_CACHE[key] = build_nc(dbg=key)
    return _NC_CACHE[key]


def make_inputs(inputs, dbg=()):
    x = np.asarray(inputs["L0_attn_x"], np.float32)
    wq = np.asarray(inputs["L0_attn_Wq"], np.float32)
    bq = np.asarray(inputs["L0_attn_bq"], np.float32)
    wk = np.asarray(inputs["L0_attn_Wk"], np.float32)
    bk = np.asarray(inputs["L0_attn_bk"], np.float32)
    wv = np.asarray(inputs["L0_attn_Wv"], np.float32)
    bv = np.asarray(inputs["L0_attn_bv"], np.float32)
    wo = np.asarray(inputs["L0_attn_Wo"], np.float32)
    bo = np.asarray(inputs["L0_attn_bo"], np.float32)
    cx = np.asarray(inputs["L0_ffn_clip_x"], np.float32)
    cd = np.asarray(inputs["L0_ffn_clip_down"], np.float32)
    wg = np.ascontiguousarray(np.asarray(inputs["L0_ffn_w_gate"], np.float32))
    wu = np.ascontiguousarray(np.asarray(inputs["L0_ffn_w_up"], np.float32))
    wd = np.ascontiguousarray(np.asarray(inputs["L0_ffn_w_down"], np.float32))

    shared = {
        "wqT": np.ascontiguousarray(wq.T), "wkT": np.ascontiguousarray(wk.T),
        "wvT": np.ascontiguousarray(wv.T), "woT": np.ascontiguousarray(wo.T),
        "bqf": np.ascontiguousarray(bq.reshape(8, 128).T),
        "bkf": np.ascontiguousarray(bk.reshape(8, 128).T),
        "bof": np.ascontiguousarray(bo.reshape(8, 128).T),
        "bvr": bv.reshape(1, D),
        "wg": wg, "wu": wu, "wd": wd,
        "clips": np.array([[cx[0], cd[0]]], np.float32),
        "ident": np.eye(128, dtype=BF16NP),
        "identf": np.eye(128, dtype=np.float32),
        "onesr": np.ones((1, 128), np.float32),
    }
    masks = [_mask_host(0), _mask_host(1)]
    in_maps = []
    for c in range(NCORE):
        b, hf = c // 2, c % 2
        lo, hi = hf * T - 2, hf * T + T + 2
        sl = x[b, max(lo, 0):min(hi, 4096)]
        if lo < 0:
            sl = np.concatenate([np.zeros((-lo, D), np.float32), sl], 0)
        if hi > 4096:
            sl = np.concatenate([sl, np.zeros((hi - 4096, D), np.float32)], 0)
        m = dict(shared)
        m["x"] = np.ascontiguousarray(sl)
        m["mask"] = masks[hf]
        in_maps.append(m)
    return in_maps


def kernel(**inputs):
    nc = _get_nc()
    in_maps = make_inputs(inputs)
    res = run_bass_kernel_spmd(nc, in_maps, core_ids=list(range(NCORE)))
    out = np.empty((4, 4096, D), np.float32)
    for c in range(NCORE):
        b, hf = c // 2, c % 2
        out[b, hf * T:hf * T + T] = res.results[c]["out"]
    return out
